# version 08
# baseline (speedup 1.0000x reference)
import math
import numpy as np
import ml_dtypes
import concourse.bass as bass
import concourse.mybir as mybir
from concourse.bass_utils import run_bass_kernel_spmd
from contextlib import ExitStack

F32 = mybir.dt.float32
BF16 = mybir.dt.bfloat16
I32 = mybir.dt.int32
AF = mybir.ActivationFunctionType
ALU = mybir.AluOpType
AX = mybir.AxisListType

D = 1024
T = 4096
NCH = 8
PASS = 512
NPASS = T // PASS
TCH = PASS // 128
NEXP = 8
EPS = 1e-5
PI = math.pi

SEMS = ("pe", "act", "dve", "d0", "d1", "wc", "ab", "wd", "x", "out", "cr", "wb")


class Ctx:
    def __init__(self, nc, stack):
        self.nc = nc
        self.stack = stack
        self.n = {s: 0 for s in SEMS}
        self.sem = {s: stack.enter_context(nc.semaphore("sem_" + s)) for s in SEMS}
        self.hw = {}
        self.wcache = {}

    def inc(self, ins, name, n=1):
        ins.then_inc(self.sem[name], n)
        self.n[name] += n
        return self.n[name]

    def wait(self, eng, label, name, val):
        if val is None or val <= 0:
            return
        key = (label, name)
        if self.hw.get(key, 0) >= val:
            return
        eng.wait_ge(self.sem[name], val)
        self.hw[key] = val


class Streams:
    def __init__(self, ctx):
        self.ctx = ctx
        self.items = {k: [] for k in ("pe", "act", "dve", "gp", "sp")}
        self.c = dict(ctx.n)
        self.ev = {}

    def mark(self, sem, key=None, n=1):
        self.c[sem] += n
        if key is not None:
            self.ev[key] = self.c[sem]
        return self.c[sem]

    def w(self, eng, sem, val):
        if val:
            self.items[eng].append(("w", sem, val))

    def op(self, eng, fn):
        self.items[eng].append(("c", fn))

    def run(self):
        ctx = self.ctx
        with ctx.nc.Block() as blk:
            def make(lbl, items):
                def body(eng):
                    for it in items:
                        if it[0] == "w":
                            ctx.wait(eng, lbl, it[1], it[2])
                        else:
                            it[1](eng)
                return body
            if self.items["pe"]:
                blk.tensor(make("pe", self.items["pe"]))
            if self.items["act"]:
                blk.scalar(make("act", self.items["act"]))
            if self.items["dve"]:
                blk.vector(make("dve", self.items["dve"]))
            if self.items["gp"]:
                blk.gpsimd(make("gp", self.items["gp"]))
            if self.items["sp"]:
                blk.sync(make("sp", self.items["sp"]))
        for s in SEMS:
            assert self.ctx.n[s] == self.c[s], (s, self.ctx.n[s], self.c[s])


def dma_cast_w(ctx, st, dram_2d, rows0, rows1, cols0, cols1, dst, dst_col0,
               sem, b16=None, wkey=None):
    ncols = cols1 - cols0
    nk = (rows1 - rows0) // 128
    ndma = 2 if nk > 4 else 1
    per = nk // ndma
    cached = wkey is not None and wkey in ctx.wcache
    if cached:
        st.w("gp", "wb", ctx.wcache[wkey])
    else:
        st.w("gp", "wb", st.c["wb"])
    wb_done = None
    for i in range(ndma):
        k0 = i * per
        kk = per
        sel = slice(rows0 + k0 * 128, rows0 + (k0 + kk) * 128)
        d = dst[:, dst_col0 + k0 * ncols: dst_col0 + (k0 + kk) * ncols]
        d = d.rearrange("p (k c) -> p k c", k=kk)
        if cached:
            src = b16[sel, cols0:cols1].rearrange("(k p) c -> p k c", p=128)
        else:
            src = dram_2d[sel, cols0:cols1].rearrange("(k p) c -> p k c", p=128)
        def fn(eng, s=src, d=d):
            ctx.inc(eng.dma_start(d, s), sem, 16)
        st.op("gp", fn)
        st.mark(sem, n=16)
        if b16 is not None and not cached:
            wdst = b16[sel, cols0:cols1].rearrange("(k p) c -> p k c", p=128)
            ld_ev = st.c[sem]
            st.w("sp", sem, ld_ev)
            def fn(eng, s=d, dd=wdst):
                ctx.inc(eng.dma_start(dd, s), "wb", 16)
            st.op("sp", fn)
            st.mark("wb", n=16)
            wb_done = st.c["wb"]
    if b16 is not None and not cached:
        ctx.wcache[wkey] = wb_done
    return st.c[sem]


def dma_flat(ctx, st, src2d, c0, c1, dst, dst0, sem, b16=None, wkey=None,
             ndma=1):
    ncols = c1 - c0
    cached = wkey is not None and wkey in ctx.wcache
    if cached:
        st.w("gp", "wb", ctx.wcache[wkey])
    else:
        st.w("gp", "wb", st.c["wb"])
    per = ncols // ndma
    wb_done = None
    for i in range(ndma):
        a0, a1 = c0 + i * per, c0 + (i + 1) * per
        d = dst[:, dst0 + i * per: dst0 + (i + 1) * per]
        s = (b16 if cached else src2d)[:, a0:a1]
        def fn(eng, s=s, d=d):
            ctx.inc(eng.dma_start(d, s), sem, 16)
        st.op("gp", fn)
        st.mark(sem, n=16)
        if b16 is not None and not cached:
            ld_ev = st.c[sem]
            st.w("sp", sem, ld_ev)
            def fn(eng, s=d, dd=b16[:, a0:a1]):
                ctx.inc(eng.dma_start(dd, s), "wb", 16)
            st.op("sp", fn)
            st.mark("wb", n=16)
            wb_done = st.c["wb"]
    if b16 is not None and not cached:
        ctx.wcache[wkey] = wb_done
    return st.c[sem]


def dma_x(ctx, st, dram_xT, p, xsb, store=False):
    for halfk in range(2):
        da = dram_xT[halfk * 512:(halfk + 1) * 512, p * PASS:(p + 1) * PASS]
        da = da.rearrange("(k p) c -> p k c", p=128)
        sl = xsb[:, halfk * 2048:(halfk + 1) * 2048].rearrange(
            "p (k c) -> p k c", k=4)
        if store:
            def fn(eng, a=da, b=sl):
                ctx.inc(eng.dma_start(a, b), "out", 16)
        else:
            def fn(eng, a=da, b=sl):
                ctx.inc(eng.dma_start(b, a), "x", 16)
        st.op("sp", fn)
        st.mark("out" if store else "x", n=16)
    return st.c["out" if store else "x"]


def emit_prologue(ctx, g, pst):
    nc = ctx.nc
    dr = g["dram"]
    sb = g
    sb["rowsf"] = pst.enter_context(nc.sbuf_tensor("rowsf", [1, 4096], F32))
    sb["prow"] = pst.enter_context(nc.sbuf_tensor("prow", [1, 3072], F32))
    sb["binrow"] = pst.enter_context(nc.sbuf_tensor("binrow", [1, 2048], F32))

    st = Streams(ctx)
    def ld(dst, src):
        def fn(eng, d=dst, s=src):
            ctx.inc(eng.dma_start(d, s), "cr", 16)
        st.op("sp", fn)
        st.mark("cr", n=16)
    ld(sb["tri"][:, :], dr["tri_bf"][:, :])
    ld(sb["iden"][:, :], dr["iden_f"][:, :])
    ld(sb["iota"][:, :], dr["iota_row"][:, :])
    ld(sb["ones1f"][:, :], dr["ones_row_f"][:, :])
    ld(sb["ones1b"][:, :], dr["ones_row_bf"][:, :])
    ld(sb["onescb"][:, :], dr["ones_col_bf"][:, :])
    ld(sb["biaspt"][:, :], dr["bias_pt"][:, :])
    ld(sb["gTs"][:, :].rearrange("p (k e) -> p k e", k=8),
       dr["gT"].rearrange("(k p) e -> p k e", p=128))
    def f0(eng):
        eng.memset(sb["c_eps"][:, 0:1], EPS)
    st.op("dve", f0)
    for e in range(3):
        ld(sb["prow"][0:1, e * D:(e + 1) * D], dr["plog"][e:e + 1, :])
    for e in range(2):
        ld(sb["binrow"][0:1, e * D:(e + 1) * D], dr["b_in"][e:e + 1, :])
    dma_x(ctx, st, dr["xT"], 0, sb["xs0"])
    st.run()
    cr0 = ctx.n["cr"]

    if True:
        pA = pst.enter_context(nc.psum_tensor("pro_a", [128, 512], F32))
        pB = pst.enter_context(nc.psum_tensor("pro_b", [128, 512], F32))
        pC = pst.enter_context(nc.psum_tensor("pro_c", [128, 512], F32))
        pD = pst.enter_context(nc.psum_tensor("pro_d", [128, 512], F32))
        tA = sb["xs1"][:, 0:1024]
        tB = sb["xs1"][:, 1024:2048]
        tC = sb["xs1"][:, 2048:3072]
        tD = sb["xs1"][:, 3072:4096]
        tI = pst.enter_context(nc.sbuf_tensor("proI", [128, 1024], F32))
        ti32 = pst.enter_context(nc.sbuf_tensor("proi32", [128, 1024], I32))

        rows = sb["rowsf"]
        nu_r = rows[0:1, 0:D]
        t2_r = rows[0:1, D:2 * D]
        gm_r = rows[0:1, 2 * D:3 * D]

        st = Streams(ctx)
        st.w("act", "cr", cr0)
        def f(s):
            s.activation(nu_r, sb["prow"][0:1, 0:D], AF.Exp)
            s.activation(t2_r, sb["prow"][0:1, D:2 * D], AF.Exp)
            s.mul(t2_r, t2_r, 1.0 / (2 * PI))
            s.activation(gm_r, sb["prow"][0:1, 2 * D:3 * D], AF.Exp)
            s.copy(sb["bbrow"][0:1, 0:D], sb["binrow"][0:1, 0:D])
            ctx.inc(s.copy(sb["bbrow"][0:1, D:2 * D], sb["binrow"][0:1, D:2 * D]),
                    "act")
        st.op("act", f)
        st.mark("act", "rows")
        st.run()

        def frac_reduce(v, t):
            for _ in range(2):
                v.tensor_copy(ti32[:, 0:1024], t[:, 0:1024])
                v.tensor_copy(tI[:, 0:1024], ti32[:, 0:1024])
                v.tensor_sub(t[:, 0:1024], t[:, 0:1024], tI[:, 0:1024])

        st = Streams(ctx)
        def f(t):
            for half in range(2):
                c0, c1 = half * 512, (half + 1) * 512
                t.matmul([pA, pB][half][:, :], sb["iota"][:, :], nu_r[:, c0:c1],
                         start=True, stop=True)
                ins = t.matmul([pC, pD][half][:, :], sb["iota"][:, :],
                               t2_r[:, c0:c1], start=True, stop=True)
            ctx.inc(ins, "pe")
        st.op("pe", f)
        st.mark("pe", "outer")
        st.w("dve", "pe", st.ev["outer"])
        def f(v):
            v.tensor_copy(tA[:, 0:512], pA[:, :])
            v.tensor_copy(tA[:, 512:1024], pB[:, :])
            v.tensor_copy(tB[:, 0:512], pC[:, :])
            v.tensor_copy(tB[:, 512:1024], pD[:, :])
            frac_reduce(v, tB)
            ctx.inc(v.tensor_copy(tI[:, 0:1], tB[:, 0:1]), "dve")
        st.op("dve", f)
        st.mark("dve", "frac")
        st.w("act", "dve", st.ev["frac"])
        def f(s):
            s.activation(tC[:, 0:1024], tB[:, 0:1024], AF.Sin, scale=2 * PI)
            s.activation(tD[:, 0:1024], tB[:, 0:1024], AF.Sin, scale=PI)
            s.activation(tD[:, 0:1024], tD[:, 0:1024], AF.Square)
            ctx.inc(s.activation(tA[:, 0:1024], tA[:, 0:1024], AF.Exp), "act")
        st.op("act", f)
        st.mark("act", "trig")
        st.run()

        st = Streams(ctx)
        def f(t):
            for half in range(2):
                ins = t.matmul([pA, pB][half][:, :], sb["ones1f"][:, :],
                               gm_r[:, half * 512:(half + 1) * 512],
                               start=True, stop=True)
            ctx.inc(ins, "pe")
        st.op("pe", f)
        st.mark("pe", "gb")
        st.w("dve", "pe", st.ev["gb"])
        def f(v):
            v.tensor_scalar(tD[:, 0:1024], tD[:, 0:1024], -2.0, 1.0,
                            ALU.mult, ALU.add)
            v.tensor_mul(tA[:, 0:512], tA[:, 0:512], pA[:, :])
            v.tensor_mul(tA[:, 512:1024], tA[:, 512:1024], pB[:, :])
            v.tensor_mul(sb["PGr"][:, :], tA[:, 0:1024], tD[:, 0:1024])
            v.tensor_mul(tI[:, 0:1024], tA[:, 0:1024], tC[:, 0:1024])
            ctx.inc(v.tensor_scalar_mul(sb["PGi"][:, :], tI[:, 0:1024], -1.0),
                    "dve")
        st.op("dve", f)
        st.mark("dve", "pg")
        st.run()

        st = Streams(ctx)
        def f(t):
            for m in range(8):
                q = [pA, pB][m // 4]
                t.matmul(q[:, (m % 4) * 128:(m % 4) * 128 + 128],
                         nu_r[:, m * 128:(m + 1) * 128], sb["iota"][:, :],
                         start=(m % 4 == 0), stop=True)
            for m in range(8):
                q = [pC, pD][m // 4]
                ins = t.matmul(q[:, (m % 4) * 128:(m % 4) * 128 + 128],
                               t2_r[:, m * 128:(m + 1) * 128], sb["iota"][:, :],
                               start=(m % 4 == 0), stop=True)
            ctx.inc(ins, "pe")
        st.op("pe", f)
        st.mark("pe", "outq")
        st.w("dve", "pe", st.ev["outq"])
        def f(v):
            v.tensor_copy(tB[:, 0:512], pC[:, :])
            v.tensor_copy(tB[:, 512:1024], pD[:, :])
            frac_reduce(v, tB)
            v.tensor_copy(tA[:, 0:512], pA[:, :])
            ctx.inc(v.tensor_copy(tA[:, 512:1024], pB[:, :]), "dve")
        st.op("dve", f)
        st.mark("dve", "fracq")
        st.w("act", "dve", st.ev["fracq"])
        def f(s):
            s.activation(tC[:, 0:1024], tB[:, 0:1024], AF.Sin, scale=2 * PI)
            s.activation(tD[:, 0:1024], tB[:, 0:1024], AF.Sin, scale=PI)
            s.activation(tD[:, 0:1024], tD[:, 0:1024], AF.Square)
            ctx.inc(s.activation(tA[:, 0:1024], tA[:, 0:1024], AF.Exp,
                                 scale=-1.0), "act")
        st.op("act", f)
        st.mark("act", "trigq")
        st.w("dve", "act", st.ev["trigq"])
        def f(v):
            v.tensor_scalar(tD[:, 0:1024], tD[:, 0:1024], -2.0, 1.0,
                            ALU.mult, ALU.add)
            v.tensor_mul(tB[:, 0:1024], tA[:, 0:1024], tD[:, 0:1024])
            v.tensor_mul(tC[:, 0:1024], tA[:, 0:1024], tC[:, 0:1024])
            v.tensor_copy(sb["Qr"][:, :], tB[:, 0:1024])
            v.tensor_copy(sb["Qi"][:, :], tC[:, 0:1024])
            v.tensor_scalar_mul(sb["Qineg"][:, :], tC[:, 0:1024], -1.0)
            s8 = sb["scr8"]
            v.tensor_copy(s8[:, 0:8], tB[:, 1:1024:128])
            v.tensor_copy(s8[:, 8:16], tC[:, 1:1024:128])
            v.tensor_copy(s8[:, 16:24], tB[:, 127:1024:128])
            v.tensor_copy(s8[:, 24:32], tC[:, 127:1024:128])
            bpt = sb["biaspt"]
            v.tensor_scalar_mul(sb["lcr"][:, 0:8], s8[:, 0:8], 1.0)
            v.tensor_mul(s8[:, 32:40], s8[:, 0:8], s8[:, 16:24])
            v.tensor_mul(s8[:, 40:48], s8[:, 8:16], s8[:, 24:32])
            v.tensor_add(sb["bo"][:, 0:8], bpt[:, 16:24], bpt[:, 24:32])
            v.tensor_sub(sb["QLr"][:, 0:8], s8[:, 32:40], s8[:, 40:48])
            v.tensor_mul(s8[:, 32:40], s8[:, 0:8], s8[:, 24:32])
            v.tensor_mul(s8[:, 40:48], s8[:, 8:16], s8[:, 16:24])
            v.tensor_sub(sb["bo"][:, 0:8], sb["bo"][:, 0:8], bpt[:, 32:40])
            v.tensor_add(sb["QLi"][:, 0:8], s8[:, 32:40], s8[:, 40:48])
            v.memset(sb["lcr"][:, 0:8], 0.0)
            ctx.inc(v.memset(sb["lci"][:, 0:8], 0.0), "dve")
        st.op("dve", f)
        st.mark("dve", "qtab")
        st.run()


def emit_ln(ctx, g, est, xs, xln, routed, ctile, pre=None, wait_x=None):
    nc = ctx.nc
    sb = g
    i = emit_ln.i
    emit_ln.i += 1
    scr = {}
    scr["rows"] = est.enter_context(nc.sbuf_tensor(f"lnr{i}", [1, 2048], F32))
    scr["sqb"] = est.enter_context(nc.sbuf_tensor(f"sqb{i}", [128, 1024], BF16))
    scr["xbb"] = est.enter_context(nc.sbuf_tensor(f"xbb{i}", [128, 1024], BF16))
    scr["zf"] = est.enter_context(nc.sbuf_tensor(f"zf{i}", [128, 2048], F32))
    with ExitStack() as pst:
        ps_sum = pst.enter_context(nc.psum_tensor(f"pss{i}", [128, 512], F32))
        ps_sq = pst.enter_context(nc.psum_tensor(f"psq{i}", [128, 512], F32))
        ps_A = pst.enter_context(nc.psum_tensor(f"psA{i}", [128, 512], F32))
        ps_B = pst.enter_context(nc.psum_tensor(f"psB{i}", [128, 512], F32))
        ps_log = None
        if routed:
            ps_log = [pst.enter_context(nc.psum_tensor(f"psl{i}_{s}", [128, 512],
                                                       F32)) for s in range(4)]
            for nme, shp in (("sc8", 32), ("g4", 32), ("msk", 32), ("comb", 32),
                             ("mx", 4), ("nmx", 4), ("m8", 64), ("gm4", 16)):
                scr[nme] = est.enter_context(
                    nc.sbuf_tensor(f"{nme}{i}", [128, shp], F32))
            scr["cTsb"] = est.enter_context(nc.sbuf_tensor(f"cT{i}", [8, 512], F32))

        st = Streams(ctx)
        if pre:
            pre(st)
        rows = scr["rows"]
        m_r, e_r = rows[0:1, 0:512], rows[0:1, 512:1024]
        A_r, B_r = rows[0:1, 1024:1536], rows[0:1, 1536:2048]
        sqb, xbb, zf = scr["sqb"], scr["xbb"], scr["zf"]
        nz = 3

        for k in range(NCH):
            sl = k % 2
            if k >= 2:
                st.w("act", "pe", st.ev[("st", k - 2)])
            if k == 0 and wait_x is not None:
                st.w("act", "x", wait_x)
            def f(s, k=k, sl=sl):
                s.activation(sqb[:, sl * 512:(sl + 1) * 512],
                             xs[:, k * 512:(k + 1) * 512], AF.Square)
                ctx.inc(s.copy(xbb[:, sl * 512:(sl + 1) * 512],
                               xs[:, k * 512:(k + 1) * 512]), "act")
            st.op("act", f)
            st.mark("act", ("sq", k))
            st.w("pe", "act", st.ev[("sq", k)])
            def f(t, k=k, sl=sl):
                t.matmul(ps_sum[0:1, :], sb["onescb"][:, :],
                         xbb[:, sl * 512:(sl + 1) * 512],
                         start=(k == 0), stop=(k == NCH - 1))
                ctx.inc(t.matmul(ps_sq[0:1, :], sb["onescb"][:, :],
                                 sqb[:, sl * 512:(sl + 1) * 512],
                                 start=(k == 0), stop=(k == NCH - 1)), "pe")
            st.op("pe", f)
            st.mark("pe", ("st", k))

        st.w("act", "pe", st.ev[("st", NCH - 1)])
        def f(s):
            s.mul(m_r, ps_sum[0:1, :], 1.0 / D)
            ctx.inc(s.mul(e_r, ps_sq[0:1, :], 1.0 / D), "act")
        st.op("act", f)
        st.mark("act", "me")
        st.w("dve", "act", st.ev["me"])
        def f(v):
            v.tensor_mul(B_r, m_r, m_r)
            ctx.inc(v.tensor_sub(e_r, e_r, B_r), "dve")
        st.op("dve", f)
        st.mark("dve", "var")
        st.w("act", "dve", st.ev["var"])
        def f(s):
            s.activation(A_r, e_r, AF.Ln, bias=EPS)
            ctx.inc(s.activation(A_r, A_r, AF.Exp, scale=-0.5), "act")
        st.op("act", f)
        st.mark("act", "A")
        st.w("dve", "act", st.ev["A"])
        def f(v):
            ctx.inc(v.scalar_tensor_tensor(B_r, m_r, -1.0, A_r, ALU.mult,
                                           ALU.mult), "dve")
        st.op("dve", f)
        st.mark("dve", "B")
        st.w("pe", "dve", st.ev["B"])
        def f(t):
            t.matmul(ps_A[:, :], sb["ones1f"][:, :], A_r, start=True, stop=True)
            ctx.inc(t.matmul(ps_B[:, :], sb["ones1f"][:, :], B_r, start=True,
                             stop=True), "pe")
        st.op("pe", f)
        st.mark("pe", "bc")

        psA3 = ps_A[:, :].rearrange("p (o c) -> p o c", o=1).broadcast_to(
            [128, 4, 512])
        psB3 = ps_B[:, :].rearrange("p (o c) -> p o c", o=1).broadcast_to(
            [128, 4, 512])
        for hf in range(2):
            x3 = xs[:, hf * 2048:(hf + 1) * 2048].rearrange(
                "p (k c) -> p k c", k=4)
            st.w("dve", "pe", st.ev["bc"])
            if hf == 1:
                st.w("dve", "act", st.ev[("xl", 0)])
                if routed:
                    st.w("dve", "pe", st.ev[("gt", 0)])
            z3 = zf[:, 0:2048].rearrange("p (k c) -> p k c", k=4)
            if routed:
                def f(v, x3=x3, z3=z3):
                    v.tensor_tensor(z3, x3, psA3, ALU.mult)
                    ctx.inc(v.tensor_tensor(z3, z3, psB3, ALU.add), "dve")
                st.op("dve", f)
                st.mark("dve", ("z", hf))
                st.w("act", "dve", st.ev[("z", hf)])
                def f(s, hf=hf):
                    ctx.inc(s.copy(xln[:, hf * 2048:(hf + 1) * 2048],
                                   zf[:, 0:2048]), "act")
                st.op("act", f)
                st.mark("act", ("xl", hf))
                st.w("pe", "dve", st.ev[("z", hf)])
                def f(t, hf=hf):
                    for k in range(4):
                        kk = hf * 4 + k
                        for sub in range(4):
                            ins = t.matmul(ps_log[sub][:, 0:8],
                                           zf[:, k * 512 + sub * 128:
                                              k * 512 + (sub + 1) * 128],
                                           sb["gTs"][:, kk * 8:(kk + 1) * 8],
                                           start=(kk == 0),
                                           stop=(kk == NCH - 1))
                    ctx.inc(ins, "pe")
                st.op("pe", f)
                st.mark("pe", ("gt", hf))
            else:
                xl3 = xln[:, hf * 2048:(hf + 1) * 2048].rearrange(
                    "p (k c) -> p k c", k=4)
                def f(v, x3=x3, z3=z3, xl3=xl3):
                    v.tensor_tensor(z3, x3, psA3, ALU.mult)
                    ctx.inc(v.tensor_tensor(xl3, z3, psB3, ALU.add), "dve")
                st.op("dve", f)
                st.mark("dve", ("z", hf))
                st.ev[("xl", hf)] = 0

        if routed:
            sc8, g4, msk = scr["sc8"], scr["g4"], scr["msk"]
            comb, mx, nmx, m8 = scr["comb"], scr["mx"], scr["nmx"], scr["m8"]
            gm4 = scr["gm4"]
            ps_cT = ps_sum
            st.w("dve", "pe", st.ev[("gt", 1)])
            def f(v):
                for sub in range(4):
                    v.reduce_max(mx[:, sub:sub + 1],
                                 ps_log[sub][:, 0:8], AX.X)
            st.op("dve", f)
            for sub in range(4):
                def f(v, sub=sub):
                    ctx.inc(v.tensor_scalar_mul(nmx[:, sub:sub + 1],
                                                mx[:, sub:sub + 1], -1.0),
                            "dve")
                st.op("dve", f)
                st.mark("dve", ("nm", sub))
            for sub in range(4):
                st.w("act", "dve", st.ev[("nm", sub)])
                def f(s, sub=sub):
                    ctx.inc(s.activation(sc8[:, sub * 8:sub * 8 + 8],
                                         ps_log[sub][:, 0:8], AF.Exp,
                                         bias=nmx[:, sub:sub + 1]), "act")
                st.op("act", f)
                st.mark("act", ("ex", sub))
            st.w("dve", "act", st.ev[("ex", 3)])
            def f(v):
                for sub in range(4):
                    v.reduce_sum(mx[:, sub:sub + 1], sc8[:, sub * 8:sub * 8 + 8],
                                 AX.X)
                for sub in range(4):
                    v.reciprocal(nmx[:, sub:sub + 1], mx[:, sub:sub + 1])
                for sub in range(4):
                    v.tensor_scalar_mul(sc8[:, sub * 8:sub * 8 + 8],
                                        sc8[:, sub * 8:sub * 8 + 8],
                                        nmx[:, sub:sub + 1])
                for sub in range(4):
                    v.memset(g4[:, sub * 8 + 4:sub * 8 + 8], 0.0)
                for sub in range(4):
                    s8 = sc8[:, sub * 8:sub * 8 + 8]
                    v.tensor_tensor(g4[:, sub * 8:sub * 8 + 4], s8[:, 0:8:2],
                                    s8[:, 1:8:2], ALU.max)
                for sub in range(4):
                    v.max(out=m8[:, sub * 16:sub * 16 + 8],
                          in_=g4[:, sub * 8:sub * 8 + 8])
                for sub in range(4):
                    v.tensor_scalar(gm4[:, sub * 4:sub * 4 + 4],
                                    g4[:, sub * 8:sub * 8 + 4],
                                    m8[:, sub * 16 + 1:sub * 16 + 2], None,
                                    ALU.is_ge)
                for sub in range(4):
                    mk = msk[:, sub * 8:sub * 8 + 8]
                    v.tensor_copy(mk[:, 0:8:2], gm4[:, sub * 4:sub * 4 + 4])
                for sub in range(4):
                    mk = msk[:, sub * 8:sub * 8 + 8]
                    v.tensor_copy(mk[:, 1:8:2], gm4[:, sub * 4:sub * 4 + 4])
                for sub in range(4):
                    mk = msk[:, sub * 8:sub * 8 + 8]
                    v.tensor_mul(mk, mk, sc8[:, sub * 8:sub * 8 + 8])
                for sub in range(4):
                    v.max(out=m8[:, sub * 16 + 8:sub * 16 + 16],
                          in_=msk[:, sub * 8:sub * 8 + 8])
                for sub in range(4):
                    v.tensor_add(mx[:, sub:sub + 1],
                                 m8[:, sub * 16 + 8:sub * 16 + 9],
                                 m8[:, sub * 16 + 9:sub * 16 + 10])
                for sub in range(4):
                    v.reciprocal(nmx[:, sub:sub + 1], mx[:, sub:sub + 1])
                for sub in range(4):
                    v.tensor_scalar(comb[:, sub * 8:sub * 8 + 8],
                                    msk[:, sub * 8:sub * 8 + 8],
                                    m8[:, sub * 16 + 9:sub * 16 + 10], None,
                                    ALU.is_ge)
                for sub in range(4):
                    v.tensor_mul(comb[:, sub * 8:sub * 8 + 8],
                                 comb[:, sub * 8:sub * 8 + 8],
                                 msk[:, sub * 8:sub * 8 + 8])
            st.op("dve", f)
            for sub in range(4):
                def f(v, sub=sub):
                    ctx.inc(v.tensor_scalar(comb[:, sub * 8:sub * 8 + 8],
                                            comb[:, sub * 8:sub * 8 + 8],
                                            nmx[:, sub:sub + 1], 0.5,
                                            ALU.mult, ALU.mult), "dve")
                st.op("dve", f)
                st.mark("dve", ("cm", sub))
            for sub in range(4):
                st.w("pe", "dve", st.ev[("cm", sub)])
                def f(t, sub=sub):
                    ctx.inc(t.matmul(ps_cT[0:8, sub * 128:(sub + 1) * 128],
                                     comb[:, sub * 8:sub * 8 + 8],
                                     sb["iden"][:, :], start=(sub == 0),
                                     stop=True), "pe")
                st.op("pe", f)
                st.mark("pe", ("cT", sub))
            st.w("pe", "dve", st.ev[("cm", 3)])
            def f(t):
                ctx.inc(t.matmul(ps_sq[0:8, 0:128], comb[:, 0:8],
                                 sb["iden"][:, :], start=True, stop=True), "pe")
            st.op("pe", f)
            st.mark("pe", "cTguard")
            st.w("dve", "pe", st.ev["cTguard"])
            def f(v):
                ctx.inc(v.tensor_copy(scr["cTsb"][0:8, 0:512],
                                      ps_cT[0:8, 0:512]), "dve")
            st.op("dve", f)
            st.mark("dve", "cte")
            st.w("gp", "dve", st.ev["cte"])
            for e in range(NEXP):
                def f(gp, e=e):
                    ctx.inc(gp.dma_start(
                        sb["crows"][0:1, e * 1024 + ctile * 512:
                                    e * 1024 + (ctile + 1) * 512],
                        scr["cTsb"][e:e + 1, 0:512]), "cr", 16)
                st.op("gp", f)
                st.mark("cr", n=16)
        st.run()
emit_ln.i = 0


def emit_ffn_exp(ctx, g, est, tiles, pre_e, w2_0, post=None):
    nc = ctx.nc
    sb = g
    dr = g["dram"]
    i = emit_ffn_exp.i
    emit_ffn_exp.i += 1
    nt = len(tiles)
    act_bufs = [est.enter_context(nc.sbuf_tensor(f"fac{i}_{t}", [128, 8192], BF16))
                for t in range(nt)]
    silu_sb = est.enter_context(nc.sbuf_tensor(f"fsl{i}", [128, 1024], F32))
    tmp2 = est.enter_context(nc.sbuf_tensor(f"ftm{i}", [128, 1024], F32))
    csb = est.enter_context(nc.sbuf_tensor(f"fcs{i}", [128, 1024], F32))
    with ExitStack() as pst:
        pp = pst.enter_context(nc.psum_tensor(f"fpp{i}", [128, 2048], F32))
        po = pst.enter_context(nc.psum_tensor(f"fpo{i}", [128, 2048], F32))

        st = Streams(ctx)
        wD, wC = sb["wD"], sb["wC"]
        crow_ev = ctx.n["cr"]

        sim = {k: ctx.n[k] for k in ("d0", "d1", "wc")}
        wait_eighth = dict(pre_e)
        wait_w2 = {0: w2_0}
        gp_plan = []
        for e in range(NEXP):
            for q in range(8):
                if (e, q) in wait_eighth:
                    continue
                sim["d%d" % (q % 2)] += 32
                wait_eighth[(e, q)] = sim["d%d" % (q % 2)]
                gp_plan.append(("e8", e, q))
            if e + 1 < NEXP:
                sim["wc"] += 64
                wait_w2[e + 1] = sim["wc"]
                gp_plan.append(("w2", e + 1))

        last = {}
        punit = {}
        pu_ctr = [0]

        def pe_cb(e):
            for ti in range(nt):
                bank = ti
                st.w("pe", "cr", crow_ev)
                if ("po", bank) in last:
                    st.w("pe", "dve", last[("po", bank)])
                def f(t, e=e, ti=ti, bank=bank):
                    ctx.inc(t.matmul(
                        po[:, bank * 512:(bank + 1) * 512], sb["ones1b"][:, :],
                        sb["crows"][0:1, e * 1024 + tiles[ti][2] * 512:
                                    e * 1024 + (tiles[ti][2] + 1) * 512],
                        start=True, stop=True), "pe")
                st.op("pe", f)
                st.mark("pe", ("cb", e, ti))
                st.w("dve", "pe", st.ev[("cb", e, ti)])
                def f(v, ti=ti, bank=bank):
                    ctx.inc(v.tensor_copy(csb[:, ti * 512:(ti + 1) * 512],
                                          po[:, bank * 512:(bank + 1) * 512]),
                            "dve")
                st.op("dve", f)
                st.mark("dve", ("cbe", e, ti))
                last[("po", bank)] = st.ev[("cbe", e, ti)]

        def pe_pairs(e):
            for m in range(16):
                q = m // 2
                for ti in range(nt):
                    pu = pu_ctr[0]
                    sslot = pu % 2
                    st.w("pe", "d%d" % (q % 2), wait_eighth[(e, q)])
                    if ("pp", sslot) in last:
                        st.w("pe", "dve", last[("pp", sslot)])
                    buf = (q % 2) * 4096
                    gbase = (m % 2) * 2048
                    xl = tiles[ti][1]
                    def f(t, buf=buf, gbase=gbase, xl=xl, sslot=sslot):
                        gcol = sslot * 1024
                        for k in range(8):
                            t.matmul(pp[:, gcol:gcol + 512],
                                     wD[:, buf + gbase + k * 128:
                                        buf + gbase + (k + 1) * 128],
                                     xl[:, k * 512:(k + 1) * 512],
                                     start=(k == 0), stop=(k == 7))
                        for k in range(8):
                            ins = t.matmul(pp[:, gcol + 512:gcol + 1024],
                                           wD[:, buf + gbase + 1024 + k * 128:
                                              buf + gbase + 1024 + (k + 1) * 128],
                                           xl[:, k * 512:(k + 1) * 512],
                                           start=(k == 0), stop=(k == 7))
                        ctx.inc(ins, "pe")
                    st.op("pe", f)
                    st.mark("pe", ("pr", pu))
                    punit[(e, m, ti)] = pu
                    st.w("dve", "pe", st.ev[("pr", pu)])
                    if e >= 1:
                        st.w("dve", "pe", st.ev[("o", e - 1, ti, 7)])
                    def f(v, sslot=sslot):
                        ctx.inc(v.tensor_copy(
                            silu_sb[:, sslot * 512:(sslot + 1) * 512],
                            pp[:, sslot * 1024:sslot * 1024 + 512]), "dve")
                    st.op("dve", f)
                    st.mark("dve", ("pc", pu))
                    st.w("act", "dve", st.ev[("pc", pu)])
                    if pu >= 2:
                        st.w("act", "dve", st.ev[("pa", pu - 2)])
                    def f(s, sslot=sslot):
                        ctx.inc(s.activation(
                            tmp2[:, sslot * 512:(sslot + 1) * 512],
                            silu_sb[:, sslot * 512:(sslot + 1) * 512], AF.Silu),
                            "act")
                    st.op("act", f)
                    st.mark("act", ("si", pu))
                    st.w("dve", "act", st.ev[("si", pu)])
                    def f(v, e=e, m=m, ti=ti, sslot=sslot):
                        v.tensor_mul(silu_sb[:, sslot * 512:(sslot + 1) * 512],
                                     tmp2[:, sslot * 512:(sslot + 1) * 512],
                                     csb[:, ti * 512:(ti + 1) * 512])
                        ctx.inc(v.tensor_mul(
                            act_bufs[ti][:, m * 512:(m + 1) * 512],
                            silu_sb[:, sslot * 512:(sslot + 1) * 512],
                            pp[:, sslot * 1024 + 512:sslot * 1024 + 1024]),
                            "dve")
                    st.op("dve", f)
                    st.mark("dve", ("pa", pu))
                    last[("pp", sslot)] = st.ev[("pa", pu)]
                    pu_ctr[0] += 1

        def pe_mm2(e):
            for ti in range(nt):
                xs = tiles[ti][0]
                for h in range(8):
                    bank = h % 4
                    st.w("pe", "wc", wait_w2[e])
                    st.w("pe", "dve", st.ev[("pa", punit[(e, 15, nt - 1)])])
                    if ("po", bank) in last:
                        st.w("pe", "dve", last[("po", bank)])
                    def f(t, ti=ti, h=h, bank=bank):
                        for mm in range(16):
                            ins = t.matmul(
                                po[:, bank * 512:(bank + 1) * 512],
                                wC[:, h * 2048 + mm * 128: h * 2048 + (mm + 1) * 128],
                                act_bufs[ti][:, mm * 512:(mm + 1) * 512],
                                start=(mm == 0), stop=(mm == 15))
                        ctx.inc(ins, "pe")
                    st.op("pe", f)
                    st.mark("pe", ("o", e, ti, h))
                    st.w("dve", "pe", st.ev[("o", e, ti, h)])
                    def f(v, ti=ti, h=h, bank=bank, xs=xs):
                        ctx.inc(v.tensor_add(xs[:, h * 512:(h + 1) * 512],
                                             xs[:, h * 512:(h + 1) * 512],
                                             po[:, bank * 512:(bank + 1) * 512]),
                                "dve")
                    st.op("dve", f)
                    st.mark("dve", ("xa", e, ti, h))
                    last[("po", bank)] = st.ev[("xa", e, ti, h)]

        for e in range(NEXP):
            pe_cb(e)
            pe_pairs(e)
            pe_mm2(e)

        for item in gp_plan:
            if item[0] == "e8":
                _, e, q = item
                pe_, pq_ = (e, q - 2) if q >= 2 else (e - 1, q + 6)
                kk = (pe_, 2 * pq_ + 1, nt - 1)
                if kk in punit:
                    st.w("gp", "pe", st.ev[("pr", punit[kk])])
                v = dma_flat(ctx, st, dr["w13"][e][q], 0, 4096, wD,
                             (q % 2) * 4096, "d%d" % (q % 2),
                             b16=dr["w13b"][e][q], wkey=("w13", e, q), ndma=2)
                assert v == wait_eighth[(e, q)], (e, q, v, wait_eighth[(e, q)])
            else:
                _, e = item
                st.w("gp", "pe", st.ev[("o", e - 1, nt - 1, 7)])
                for j in range(4):
                    v = dma_flat(ctx, st, dr["w2t"][e], j * 4096, (j + 1) * 4096,
                                 wC, j * 4096, "wc",
                                 b16=dr["w2b"][e], wkey=("w2", e, j))
                assert v == wait_w2[e], (e, v, wait_w2[e])
        if post:
            post(st)
        st.run()
        return st.ev[("xa", NEXP - 1, 0, 7)]
emit_ffn_exp.i = 0


def attn_scratch(ctx, est, xl_nxt, xl_cur):
    nc = ctx.nc
    i = attn_scratch.i
    attn_scratch.i += 1
    scr = {"i": i}
    for nme, shape, dt in (("out_fm", [128, 4096], F32),
                           ("ytmp", [128, 2048], F32)):
        scr[nme] = est.enter_context(nc.sbuf_tensor(f"a{nme}{i}", shape, dt))
    scr["out_bf"] = xl_nxt
    scr["y1b"] = xl_cur
    return scr
attn_scratch.i = 0


def emit_attn_A(ctx, g, scr, xln, win_ab, win_wd):
    nc = ctx.nc
    sb = g
    i = scr["i"]
    with ExitStack() as pst:
        pu_ps = pst.enter_context(nc.psum_tensor(f"apu{i}", [128, 2048], F32))
        pS = pst.enter_context(nc.psum_tensor(f"apS{i}", [128, 2048], F32))
        vbf = pst.enter_context(nc.sbuf_tensor(f"avbf{i}", [128, 2048], BF16))
        vt = scr["out_fm"][:, 0:2048]
        st = Streams(ctx)
        hr, hin = scr["hr"], scr["hin"]
        wAB, wD = sb["wAB"], sb["wD"]
        for c in range(TCH):
            st.w("pe", "ab", win_ab)
            st.w("pe", "wd", win_wd)
            if c >= 1:
                st.w("pe", "dve", st.ev[("v", c - 1)])
            def f(t, c=c):
                for e2 in range(2):
                    wbuf = wAB if e2 == 0 else wD
                    for half in range(2):
                        bank = e2 * 2 + half
                        dst = pu_ps[:, bank * 512:(bank + 1) * 512]
                        for k in range(8):
                            t.matmul(dst,
                                     xln[:, k * 512 + c * 128:
                                         k * 512 + (c + 1) * 128],
                                     wbuf[:, k * 1024 + half * 512:
                                          k * 1024 + (half + 1) * 512],
                                     start=(k == 0), stop=False)
                        ins = t.matmul(dst, sb["ones1b"][:, :],
                                       sb["bbrow"][0:1, e2 * D + half * 512:
                                                   e2 * D + (half + 1) * 512],
                                       start=False, stop=True)
                ctx.inc(ins, "pe")
            st.op("pe", f)
            st.mark("pe", ("ug", c))
            st.w("dve", "pe", st.ev[("ug", c)])
            if c >= 1:
                st.w("dve", "pe", st.ev[("S", c - 1)])
            def f(v, c=c):
                for half in range(2):
                    h0 = half * 512
                    v.tensor_mul(vt[:, 0:512], sb["PGr"][:, h0:h0 + 512],
                                 pu_ps[:, h0:h0 + 512])
                    v.tensor_mul(vt[:, 512:1024], sb["PGi"][:, h0:h0 + 512],
                                 pu_ps[:, 1024 + h0:1024 + h0 + 512])
                    v.tensor_sub(vbf[:, h0:h0 + 512], vt[:, 0:512],
                                 vt[:, 512:1024])
                    v.tensor_mul(vt[:, 0:512], sb["PGi"][:, h0:h0 + 512],
                                 pu_ps[:, h0:h0 + 512])
                    v.tensor_mul(vt[:, 512:1024], sb["PGr"][:, h0:h0 + 512],
                                 pu_ps[:, 1024 + h0:1024 + h0 + 512])
                    ins = v.tensor_add(vbf[:, 1024 + h0:1024 + h0 + 512],
                                       vt[:, 0:512], vt[:, 512:1024])
                ctx.inc(ins, "dve")
            st.op("dve", f)
            st.mark("dve", ("v", c))
            st.w("pe", "dve", st.ev[("v", c)])
            if c >= 1:
                st.w("pe", "dve", st.ev[("lc", c - 1)])
            def f(t, c=c):
                for comp in range(2):
                    for m in range(8):
                        bank = comp * 2 + m // 4
                        c0 = bank * 512 + (m % 4) * 128
                        ins = t.matmul(pS[:, c0:c0 + 128],
                                       vbf[:, comp * 1024 + m * 128:
                                           comp * 1024 + (m + 1) * 128],
                                       sb["tri"][:, :], start=(m % 4 == 0),
                                       stop=True)
                ctx.inc(ins, "pe")
            st.op("pe", f)
            st.mark("pe", ("S", c))
            st.w("dve", "pe", st.ev[("S", c)])
            hr3 = hr[:, :].rearrange("p (m u) -> p m u", m=8)[
                :, :, c * 128:(c + 1) * 128]
            hi3 = hin[:, :].rearrange("p (m u) -> p m u", m=8)[
                :, :, c * 128:(c + 1) * 128]
            def f(v, c=c, hr3=hr3, hi3=hi3):
                s8 = sb["scr8"]
                pr3 = pS[:, 0:1024].rearrange("p (m t) -> p m t", m=8)
                pi3 = pS[:, 1024:2048].rearrange("p (m t) -> p m t", m=8)
                lc3r = sb["lcr"][:, 0:8].rearrange(
                    "p (m o) -> p m o", o=1).broadcast_to([128, 8, 128])
                lc3i = sb["lci"][:, 0:8].rearrange(
                    "p (m o) -> p m o", o=1).broadcast_to([128, 8, 128])
                vt0 = vt[:, 0:1024].rearrange("p (m t) -> p m t", m=8)
                vt1 = vt[:, 1024:2048].rearrange("p (m t) -> p m t", m=8)
                v.tensor_tensor(pr3, pr3, lc3r, ALU.add)
                v.tensor_tensor(pi3, pi3, lc3i, ALU.add)
                v.tensor_mul(vt[:, 0:1024], pS[:, 0:1024], sb["Qr"][:, :])
                v.tensor_mul(vt[:, 1024:2048], pS[:, 1024:2048], sb["Qi"][:, :])
                v.tensor_copy(s8[:, 32:40], pS[:, 127:1024:128])
                v.tensor_copy(s8[:, 40:48], pS[:, 1151:2048:128])
                v.tensor_sub(hr3, vt0, vt1)
                v.tensor_mul(vt[:, 0:1024], pS[:, 0:1024], sb["Qineg"][:, :])
                v.tensor_mul(vt[:, 1024:2048], pS[:, 1024:2048], sb["Qr"][:, :])
                ins = v.tensor_sub(hi3, vt0, vt1)
                ctx.inc(ins, "dve")
            st.op("dve", f)
            st.mark("dve", ("h", c))
            def f(v, c=c):
                s8 = sb["scr8"]
                v.tensor_mul(s8[:, 0:8], sb["QLr"][:, 0:8], s8[:, 32:40])
                v.tensor_mul(s8[:, 8:16], sb["QLi"][:, 0:8], s8[:, 40:48])
                v.tensor_mul(s8[:, 16:24], sb["QLr"][:, 0:8], s8[:, 40:48])
                v.tensor_mul(s8[:, 24:32], sb["QLi"][:, 0:8], s8[:, 32:40])
                v.tensor_sub(sb["lcr"][:, 0:8], s8[:, 0:8], s8[:, 8:16])
                ctx.inc(v.tensor_add(sb["lci"][:, 0:8], s8[:, 16:24],
                                     s8[:, 24:32]), "dve")
            st.op("dve", f)
            st.mark("dve", ("lc", c))
        st.run()


def emit_attn_B1(ctx, g, scr, xln, wmid_ev, pre=None):
    nc = ctx.nc
    sb = g
    with ExitStack() as pst:
        po2 = pst.enter_context(nc.psum_tensor(f"apo{scr['i']}",
                                               [128, 1024], F32))
        st = Streams(ctx)
        if pre:
            pre(st)
        win2_ev = scr["win2_ev"]
        hr, hin = scr["hr"], scr["hin"]
        out_fm, out_bf = scr["out_fm"], scr["out_bf"]
        wC, wD = sb["wC"], sb["wE"]
        for j in range(8):
            bank = j % 2
            st.w("pe", "wc", wmid_ev)
            st.w("pe", "wd", win2_ev)
            if j >= 2:
                st.w("pe", "act", st.ev[("oe", j - 2)])
            def f(t, j=j, bank=bank):
                dst = po2[:, bank * 512:(bank + 1) * 512]
                for k in range(8):
                    t.matmul(dst, wC[:, j * 1024 + k * 128: j * 1024 + (k + 1) * 128],
                             hr[:, k * 512:(k + 1) * 512], start=(k == 0),
                             stop=False)
                for k in range(8):
                    t.matmul(dst, wC[:, 8192 + j * 1024 + k * 128:
                                    8192 + j * 1024 + (k + 1) * 128],
                             hin[:, k * 512:(k + 1) * 512], start=False,
                             stop=False)
                for k in range(8):
                    ins = t.matmul(dst,
                                   wD[:, j * 1024 + k * 128: j * 1024 + (k + 1) * 128],
                                   xln[:, k * 512:(k + 1) * 512], start=False,
                                   stop=(k == 7))
                ctx.inc(ins, "pe")
            st.op("pe", f)
            st.mark("pe", ("og", j))
            st.w("dve", "pe", st.ev[("og", j)])
            def f(v, j=j, bank=bank):
                ctx.inc(v.tensor_scalar_add(out_fm[:, j * 512:(j + 1) * 512],
                                            po2[:, bank * 512:(bank + 1) * 512],
                                            sb["bo"][:, j:j + 1]), "dve")
            st.op("dve", f)
            st.mark("dve", ("od", j))
            st.w("act", "dve", st.ev[("od", j)])
            def f(s, j=j):
                ctx.inc(s.copy(out_bf[:, j * 512:(j + 1) * 512],
                               out_fm[:, j * 512:(j + 1) * 512]), "act")
            st.op("act", f)
            st.mark("act", ("oe", j))
        st.run()


def emit_attn_B2(ctx, g, scr, xs, w1_ev, pre=None):
    nc = ctx.nc
    sb = g
    i = scr["i"]
    bpt = sb["biaspt"]
    with ExitStack() as pst:
        py = pst.enter_context(nc.psum_tensor(f"apy{i}", [128, 1024], F32))
        st = Streams(ctx)
        if pre:
            pre(st)
        w2_ev = scr["w2w_ev"]
        out_fm, out_bf, y1b = scr["out_fm"], scr["out_bf"], scr["y1b"]
        ytmp = scr["ytmp"]
        wAB, wC = sb["wAB"], sb["wC"]
        for j in range(8):
            bank = j % 2
            st.w("pe", "ab", w1_ev)
            if j >= 2:
                st.w("pe", "act", st.ev[("g", j - 2)])
            def f(t, j=j, bank=bank):
                dst = py[:, bank * 512:(bank + 1) * 512]
                for k in range(8):
                    ins = t.matmul(dst,
                                   wAB[:, j * 1024 + k * 128: j * 1024 + (k + 1) * 128],
                                   out_bf[:, k * 512:(k + 1) * 512],
                                   start=(k == 0), stop=(k == 7))
                ctx.inc(ins, "pe")
            st.op("pe", f)
            st.mark("pe", ("y1", j))
            st.w("dve", "pe", st.ev[("y1", j)])
            def f(v, j=j, bank=bank):
                ctx.inc(v.tensor_scalar_add(ytmp[:, (j % 2) * 512:(j % 2 + 1) * 512],
                                            py[:, bank * 512:(bank + 1) * 512],
                                            bpt[:, 40 + j:40 + j + 1]), "dve")
            st.op("dve", f)
            st.mark("dve", ("y1d", j))
            st.w("act", "dve", st.ev[("y1d", j)])
            def f(s, j=j):
                ctx.inc(s.activation(y1b[:, j * 512:(j + 1) * 512],
                                     ytmp[:, (j % 2) * 512:(j % 2 + 1) * 512],
                                     AF.Gelu), "act")
            st.op("act", f)
            st.mark("act", ("g", j))
        for j in range(8):
            bank = j % 2
            st.w("pe", "wc", w2_ev)
            st.w("pe", "act", st.ev[("g", 7)])
            if j >= 2:
                st.w("pe", "dve", st.ev[("ye", j - 2)])
            def f(t, j=j, bank=bank):
                dst = py[:, bank * 512:(bank + 1) * 512]
                for k in range(8):
                    ins = t.matmul(dst,
                                   wC[:, j * 1024 + k * 128: j * 1024 + (k + 1) * 128],
                                   y1b[:, k * 512:(k + 1) * 512],
                                   start=(k == 0), stop=(k == 7))
                ctx.inc(ins, "pe")
            st.op("pe", f)
            st.mark("pe", ("y2", j))
            st.w("dve", "pe", st.ev[("y2", j)])
            def f(v, j=j, bank=bank):
                v.tensor_scalar_add(ytmp[:, (j % 2) * 512:(j % 2 + 1) * 512],
                                    py[:, bank * 512:(bank + 1) * 512],
                                    bpt[:, 48 + j:48 + j + 1])
                ctx.inc(v.tensor_add(out_fm[:, j * 512:(j + 1) * 512],
                                     out_fm[:, j * 512:(j + 1) * 512],
                                     ytmp[:, (j % 2) * 512:(j % 2 + 1) * 512]),
                        "dve")
            st.op("dve", f)
            st.mark("dve", ("yd", j))
            st.ev[("ye", j)] = st.ev[("yd", j)]
        st.run()

    with ExitStack() as pst:
        ps_sum = pst.enter_context(nc.psum_tensor(f"bls{i}", [128, 512], F32))
        ps_sq = pst.enter_context(nc.psum_tensor(f"blq{i}", [128, 512], F32))
        ps_A = pst.enter_context(nc.psum_tensor(f"blA{i}", [128, 512], F32))
        ps_B = pst.enter_context(nc.psum_tensor(f"blB{i}", [128, 512], F32))
        sqb = pst.enter_context(nc.sbuf_tensor(f"bsq{i}", [128, 1024], BF16))
        xbb = pst.enter_context(nc.sbuf_tensor(f"bxb{i}", [128, 1024], BF16))
        rows = pst.enter_context(nc.sbuf_tensor(f"brw{i}", [1, 2048], F32))
        st = Streams(ctx)
        out_fm = scr["out_fm"]
        ytmp = scr["ytmp"]
        m_r, e_r = rows[0:1, 0:512], rows[0:1, 512:1024]
        A_r, B_r = rows[0:1, 1024:1536], rows[0:1, 1536:2048]
        for k in range(NCH):
            sl = k % 2
            if k >= 2:
                st.w("act", "pe", st.ev[("st", k - 2)])
            def f(s, k=k, sl=sl):
                s.activation(sqb[:, sl * 512:(sl + 1) * 512],
                             out_fm[:, k * 512:(k + 1) * 512], AF.Square)
                ctx.inc(s.copy(xbb[:, sl * 512:(sl + 1) * 512],
                               out_fm[:, k * 512:(k + 1) * 512]), "act")
            st.op("act", f)
            st.mark("act", ("sq", k))
            st.w("pe", "act", st.ev[("sq", k)])
            def f(t, k=k, sl=sl):
                t.matmul(ps_sum[0:1, :], sb["onescb"][:, :],
                         xbb[:, sl * 512:(sl + 1) * 512], start=(k == 0),
                         stop=(k == NCH - 1))
                ctx.inc(t.matmul(ps_sq[0:1, :], sb["onescb"][:, :],
                                 sqb[:, sl * 512:(sl + 1) * 512], start=(k == 0),
                                 stop=(k == NCH - 1)), "pe")
            st.op("pe", f)
            st.mark("pe", ("st", k))
        st.w("act", "pe", st.ev[("st", NCH - 1)])
        def f(s):
            s.mul(m_r, ps_sum[0:1, :], 1.0 / D)
            ctx.inc(s.mul(e_r, ps_sq[0:1, :], 1.0 / D), "act")
        st.op("act", f)
        st.mark("act", "me")
        st.w("dve", "act", st.ev["me"])
        def f(v):
            v.tensor_mul(B_r, m_r, m_r)
            ctx.inc(v.tensor_sub(e_r, e_r, B_r), "dve")
        st.op("dve", f)
        st.mark("dve", "var")
        st.w("act", "dve", st.ev["var"])
        def f(s):
            s.activation(A_r, e_r, AF.Ln, bias=EPS)
            ctx.inc(s.activation(A_r, A_r, AF.Exp, scale=-0.5), "act")
        st.op("act", f)
        st.mark("act", "A")
        st.w("dve", "act", st.ev["A"])
        def f(v):
            ctx.inc(v.scalar_tensor_tensor(B_r, m_r, -1.0, A_r, ALU.mult,
                                           ALU.mult), "dve")
        st.op("dve", f)
        st.mark("dve", "B")
        st.w("pe", "dve", st.ev["B"])
        def f(t):
            t.matmul(ps_A[:, :], sb["ones1f"][:, :], A_r, start=True, stop=True)
            ctx.inc(t.matmul(ps_B[:, :], sb["ones1f"][:, :], B_r, start=True,
                             stop=True), "pe")
        st.op("pe", f)
        st.mark("pe", "bc")
        psA3 = ps_A[:, :].rearrange("p (o c) -> p o c", o=1).broadcast_to(
            [128, 4, 512])
        psB3 = ps_B[:, :].rearrange("p (o c) -> p o c", o=1).broadcast_to(
            [128, 4, 512])
        for hf in range(2):
            cs = slice(hf * 2048, (hf + 1) * 2048)
            x3 = out_fm[:, cs].rearrange("p (k c) -> p k c", k=4)
            y3 = ytmp[:, 0:2048].rearrange("p (k c) -> p k c", k=4)
            xs3 = xs[:, cs].rearrange("p (k c) -> p k c", k=4)
            g3 = bpt[:, 56 + hf * 4:56 + hf * 4 + 4].rearrange(
                "p (k o) -> p k o", o=1).broadcast_to([128, 4, 512])
            b3 = bpt[:, 64 + hf * 4:64 + hf * 4 + 4].rearrange(
                "p (k o) -> p k o", o=1).broadcast_to([128, 4, 512])
            st.w("dve", "pe", st.ev["bc"])
            def f(v, x3=x3, y3=y3, xs3=xs3, g3=g3, b3=b3):
                v.tensor_tensor(y3, x3, psA3, ALU.mult)
                v.tensor_tensor(y3, y3, psB3, ALU.add)
                v.tensor_tensor(y3, y3, g3, ALU.mult)
                v.tensor_tensor(y3, y3, b3, ALU.add)
                ctx.inc(v.tensor_tensor(xs3, xs3, y3, ALU.add), "dve")
            st.op("dve", f)
            st.mark("dve", ("xa", hf))
        st.run()


def build(taps=()):
    taps = set(taps)
    nc = bass.Bass("TRN2", target_bir_lowering=False)
    stack = ExitStack()
    ctx = Ctx(nc, stack)
    g = {}
    dr = {}
    g["dram"] = dr

    dr["xT"] = nc.dram_tensor("xT", [D, T], F32, kind="ExternalInput").ap()
    w13h = nc.dram_tensor("w13", [NEXP, 8, 128, 4096], F32,
                          kind="ExternalInput").ap()
    dr["w13"] = [[w13h[e][q] for q in range(8)] for e in range(NEXP)]
    w2h = nc.dram_tensor("w2t", [NEXP, 128, 16384], F32, kind="ExternalInput").ap()
    dr["w2t"] = [w2h[e] for e in range(NEXP)]
    winh = nc.dram_tensor("win", [2, D, D], F32, kind="ExternalInput").ap()
    dr["win"] = [winh[e] for e in range(2)]
    lh = nc.dram_tensor("wlh", [6, 128, 8192], F32, kind="ExternalInput").ap()
    dr["win2"] = lh[0]
    dr["wmid"] = [lh[1], lh[2]]
    dr["w1w"] = lh[3]
    dr["w2w"] = lh[4]
    dr["gT"] = nc.dram_tensor("gT", [D, NEXP], F32, kind="ExternalInput").ap()
    dr["plog"] = nc.dram_tensor("plog", [3, D], F32, kind="ExternalInput").ap()
    dr["b_in"] = nc.dram_tensor("b_in", [3, D], F32, kind="ExternalInput").ap()
    dr["bias_pt"] = nc.dram_tensor("bias_pt", [128, 72], F32,
                                   kind="ExternalInput").ap()
    dr["tri_bf"] = nc.dram_tensor("tri_bf", [128, 128], BF16,
                                  kind="ExternalInput").ap()
    dr["iden_f"] = nc.dram_tensor("iden_f", [128, 128], F32,
                                  kind="ExternalInput").ap()
    dr["iota_row"] = nc.dram_tensor("iota_row", [1, 128], F32,
                                    kind="ExternalInput").ap()
    dr["ones_row_f"] = nc.dram_tensor("ones_row_f", [1, 128], F32,
                                      kind="ExternalInput").ap()
    dr["ones_row_bf"] = nc.dram_tensor("ones_row_bf", [1, 128], BF16,
                                       kind="ExternalInput").ap()
    dr["ones_col_bf"] = nc.dram_tensor("ones_col_bf", [128, 1], BF16,
                                       kind="ExternalInput").ap()
    dr["yT"] = nc.dram_tensor("yT", [D, T], F32, kind="ExternalOutput").ap()
    w13bh = nc.dram_tensor("w13b", [NEXP, 8, 128, 4096], BF16,
                           kind="Internal").ap()
    dr["w13b"] = [[w13bh[e][q] for q in range(8)] for e in range(NEXP)]
    w2bh = nc.dram_tensor("w2b", [NEXP, 128, 16384], BF16, kind="Internal").ap()
    dr["w2b"] = [w2bh[e] for e in range(NEXP)]
    winbh = nc.dram_tensor("winb", [2, D, D], BF16, kind="Internal").ap()
    dr["winb"] = [winbh[e] for e in range(2)]
    lbh = nc.dram_tensor("wlb", [6, 128, 8192], BF16, kind="Internal").ap()
    dr["win2b"] = lbh[0]
    dr["wmidb"] = [lbh[1], lbh[2]]
    dr["w1wb"] = lbh[3]
    dr["w2wb"] = lbh[4]

    tap_out = {}
    TAPS = {
        "PGr": [128, 1024, BF16], "PGi": [128, 1024, BF16],
        "Qr": [128, 1024, BF16], "Qi": [128, 1024, BF16],
        "QLr": [128, 8, F32], "QLi": [128, 8, F32],
        "ln0": [128, 4096, BF16], "crows0": [1, 8192, BF16],
        "postffn1": [128, 4096, F32], "h0": [128, 4096, BF16],
        "hi0": [128, 4096, BF16], "out0": [128, 4096, F32],
        "postattn0": [128, 4096, F32], "postp0": [128, 4096, F32],
    }
    for name in taps:
        shp = TAPS[name]
        tap_out[name] = nc.dram_tensor("tap_" + name, shp[:2], shp[2],
                                       kind="ExternalOutput").ap()

    def tap(name, src):
        if name not in taps:
            return
        st = Streams(ctx)
        def f(eng, s=src, d=tap_out[name]):
            ctx.inc(eng.dma_start(d, s), "out", 16)
        st.op("sp", f)
        st.mark("out", n=16)
        st.run()

    def S(name, shape, dtype):
        g[name] = stack.enter_context(nc.sbuf_tensor(name, shape, dtype))
        return g[name]

    S("xs0", [128, 4096], F32)
    S("xs1", [128, 4096], F32)
    S("xl0", [128, 4096], BF16)
    S("xl1", [128, 4096], BF16)
    S("wAB", [128, 8192], BF16)
    S("wC", [128, 16384], BF16)
    S("wD", [128, 8192], BF16)
    S("wE", [128, 8192], BF16)
    S("PGr", [128, 1024], BF16)
    S("PGi", [128, 1024], BF16)
    S("Qr", [128, 1024], BF16)
    S("Qi", [128, 1024], BF16)
    S("Qineg", [128, 1024], BF16)
    S("tri", [128, 128], BF16)
    S("iden", [128, 128], F32)
    S("iota", [1, 128], F32)
    S("ones1f", [1, 128], F32)
    S("ones1b", [1, 128], BF16)
    S("onescb", [128, 1], BF16)
    S("biaspt", [128, 72], F32)
    S("gTs", [128, 64], F32)
    S("crows", [1, 8192], BF16)
    S("bbrow", [1, 2048], BF16)
    S("scr8", [128, 48], F32)
    S("QLr", [128, 8], F32)
    S("QLi", [128, 8], F32)
    S("lcr", [128, 8], F32)
    S("lci", [128, 8], F32)
    S("bo", [128, 8], F32)
    S("c_eps", [128, 1], F32)
    nc.const_aps.register(F32, EPS, g["c_eps"][:, 0:1])

    st = Streams(ctx)
    pre_e = {}
    pre_e[(0, 0)] = dma_flat(ctx, st, dr["w13"][0][0], 0, 4096, g["wD"], 0,
                             "d0", ndma=2)
    pre_e[(0, 1)] = dma_flat(ctx, st, dr["w13"][0][1], 0, 4096, g["wD"], 4096,
                             "d1", ndma=2)
    for j in range(4):
        w2_0 = dma_flat(ctx, st, dr["w2t"][0], j * 4096, (j + 1) * 4096,
                        g["wC"], j * 4096, "wc")
    win0_ev = dma_cast_w(ctx, st, dr["win"][0], 0, 1024, 0, 1024, g["wAB"], 0,
                         "ab")
    win2_ev_g = dma_flat(ctx, st, dr["win2"], 0, 8192, g["wE"], 0, "wd",
                         ndma=2)
    st.run()

    with ExitStack() as prost:
        emit_prologue(ctx, g, prost)
    for nm in ("PGr", "PGi", "Qr", "Qi", "QLr", "QLi"):
        tap(nm, g[nm][:, :] if nm not in ("QLr", "QLi") else g[nm][:, 0:8])

    xs = [g["xs0"], g["xs1"]]
    xl = [g["xl0"], g["xl1"]]
    load_ev = {0: ctx.n["x"]}
    store_ev = {}

    with ExitStack() as est:
        emit_ln(ctx, g, est, xs[0], xl[0], routed=True, ctile=0,
                wait_x=load_ev[0])
    tap("ln0", g["xl0"][:, :])
    tap("crows0", g["crows"][0:1, :])
    with ExitStack() as est:
        emit_ffn_exp(ctx, g, est, [(xs[0], xl[0], 0)], pre_e, w2_0)
    tap("postffn1", g["xs0"][:, :])

    for p in range(NPASS):
        cur, nxt = p % 2, (p + 1) % 2
        with ExitStack() as est:
            state = {}
            def pre(st_):
                state["win1"] = dma_cast_w(ctx, st_, dr["win"][1], 0, 1024, 0,
                                           1024, g["wD"], 0, "wd",
                                           b16=dr["winb"][1], wkey=("win", 1))
                dma_flat(ctx, st_, dr["wmid"][0], 0, 8192, g["wC"], 0, "wc",
                           b16=dr["wmidb"][0], wkey=("wmid", 0), ndma=2)
                state["wmid"] = dma_flat(ctx, st_, dr["wmid"][1], 0, 8192, g["wC"], 8192, "wc",
                                           b16=dr["wmidb"][1], wkey=("wmid", 1),
                                           ndma=2)
                if p + 1 < NPASS:
                    st_.w("sp", "out", store_ev.get(p - 1))
                    load_ev[p + 1] = dma_x(ctx, st_, dr["xT"], p + 1, xs[nxt])
            with ExitStack() as lnst:
                emit_ln(ctx, g, lnst, xs[cur], xl[cur], routed=False, ctile=0,
                        pre=pre)
            ascr = attn_scratch(ctx, est, xl[nxt], xl[cur])
            with ExitStack() as hst:
                for nme in ("hr", "hin"):
                    ascr[nme] = hst.enter_context(
                        nc.sbuf_tensor(f"a{nme}{ascr['i']}", [128, 4096], BF16))
                emit_attn_A(ctx, g, ascr, xl[cur], win0_ev, state["win1"])
                if p == 0:
                    tap("h0", ascr["hr"][:, :])
                    tap("hi0", ascr["hin"][:, :])
                def pre_b1(st_):
                    scrd = ascr
                    scrd["win2_ev"] = win2_ev_g
                    scrd["w1_ev"] = dma_flat(ctx, st_, dr["w1w"], 0, 8192, g["wAB"], 0, "ab",
                                               b16=dr["w1wb"], wkey=("w1w",),
                                               ndma=2)
                emit_attn_B1(ctx, g, ascr, xl[cur], state["wmid"], pre=pre_b1)
                if p == 0:
                    tap("out0", ascr["out_fm"][:, :])
            nxt_pre = {}
            def pre_b2(st_):
                ascr["w2w_ev"] = dma_flat(ctx, st_, dr["w2w"], 0, 8192, g["wC"], 0, "wc",
                                            b16=dr["w2wb"], wkey=("w2w",),
                                            ndma=2)
                nxt_pre[(0, 0)] = dma_flat(ctx, st_, dr["w13"][0][0], 0, 4096,
                                           g["wD"], 0, "d0",
                                           b16=dr["w13b"][0][0],
                                           wkey=("w13", 0, 0), ndma=2)
                nxt_pre[(0, 1)] = dma_flat(ctx, st_, dr["w13"][0][1], 0, 4096,
                                           g["wD"], 4096, "d1",
                                           b16=dr["w13b"][0][1],
                                           wkey=("w13", 0, 1), ndma=2)
            emit_attn_B2(ctx, g, ascr, xs[cur], ascr["w1_ev"], pre=pre_b2)
            if p == 0:
                tap("postattn0", g["xs0"][:, :])

        with ExitStack() as est:
            w2_0n = [None]
            def pre2(st_):
                for j in range(4):
                    w2_0n[0] = dma_flat(ctx, st_, dr["w2t"][0], j * 4096,
                                        (j + 1) * 4096, g["wC"], j * 4096,
                                        "wc", b16=dr["w2b"][0],
                                        wkey=("w2", 0, j))
            with ExitStack() as lnst:
                emit_ln(ctx, g, lnst, xs[cur], xl[cur], routed=True, ctile=0,
                        pre=pre2)
            tiles = [(xs[cur], xl[cur], 0)]
            if p + 1 < NPASS:
                with ExitStack() as lnst:
                    emit_ln(ctx, g, lnst, xs[nxt], xl[nxt], routed=True,
                            ctile=1, wait_x=load_ev[p + 1])
                tiles.append((xs[nxt], xl[nxt], 1))
            win0n = [None]
            def post(st_):
                st_.w("sp", "dve", st_.ev[("xa", NEXP - 1, 0, 7)])
                store_ev[p] = dma_x(ctx, st_, dr["yT"], p, xs[cur], store=True)
                if p + 1 < NPASS:
                    win0n[0] = dma_cast_w(ctx, st_, dr["win"][0], 0, 1024, 0,
                                          1024, g["wAB"], 0, "ab",
                                          b16=dr["winb"][0], wkey=("win", 0))
            emit_ffn_exp(ctx, g, est, tiles, nxt_pre, w2_0n[0], post=post)
            if win0n[0] is not None:
                win0_ev = win0n[0]
        if p == 0:
            tap("postp0", g["xs0"][:, :])

    st = Streams(ctx)
    def f(sp):
        sp.wait_ge(ctx.sem["out"], ctx.n["out"])
    st.op("sp", f)
    st.run()
    stack.close()
    return nc


def prep_inputs(L0_attn_x, L0_attn_W_in, L0_attn_b_in, L0_attn_params_log,
                L0_attn_W_mid, L0_attn_b_mid, L0_attn_w1_w, L0_attn_w1_b,
                L0_attn_w2_w, L0_attn_w2_b, L0_attn_ln_g, L0_attn_ln_b,
                L0_ffn_gate_w, L0_ffn_w13_weight, L0_ffn_w2_weight):
    f = lambda a: np.asarray(a, dtype=np.float32)
    x = f(L0_attn_x)
    w13 = np.ascontiguousarray(np.transpose(f(L0_ffn_w13_weight), (0, 2, 1)))
    order = []
    for m in range(16):
        order.extend(range(m * 128, (m + 1) * 128))
        order.extend(range(2048 + m * 128, 2048 + (m + 1) * 128))
    w13 = w13[:, :, order]
    w13 = w13.reshape(8, 8, 128, 8, 2, 2, 128).transpose(0, 3, 2, 4, 5, 1, 6)
    w13 = np.ascontiguousarray(w13.reshape(8, 8, 128, 4096))
    w2t = np.transpose(f(L0_ffn_w2_weight), (0, 2, 1))
    w2t = w2t.reshape(8, 16, 128, 8, 128).transpose(0, 2, 3, 1, 4)
    w2t = np.ascontiguousarray(w2t.reshape(8, 128, 16384))

    def lhsTlay(W):
        return np.ascontiguousarray(
            W.reshape(8, 128, 8, 128).transpose(1, 2, 0, 3).reshape(128, 8192))
    b_in = f(L0_attn_b_in)
    b_mid = f(L0_attn_b_mid)
    bias_pt = np.zeros((128, 72), np.float32)
    vecs = [b_in[0], b_in[1], b_in[2], b_mid[0], b_mid[1], f(L0_attn_w1_b),
            f(L0_attn_w2_b), f(L0_attn_ln_g), f(L0_attn_ln_b)]
    for i, v in enumerate(vecs):
        bias_pt[:, i * 8:(i + 1) * 8] = v.reshape(8, 128).T
    wlh = np.stack([
        lhsTlay(f(L0_attn_W_in)[2]),
        lhsTlay(f(L0_attn_W_mid)[0]),
        lhsTlay(f(L0_attn_W_mid)[1]),
        lhsTlay(f(L0_attn_w1_w)),
        lhsTlay(f(L0_attn_w2_w)),
        np.zeros((128, 8192), np.float32),
    ])
    shared = dict(
        w13=w13, w2t=w2t,
        win=np.ascontiguousarray(f(L0_attn_W_in)[0:2]),
        wlh=wlh,
        gT=np.ascontiguousarray(f(L0_ffn_gate_w).T),
        plog=np.ascontiguousarray(f(L0_attn_params_log)),
        b_in=np.ascontiguousarray(b_in),
        bias_pt=bias_pt,
        tri_bf=np.triu(np.ones((128, 128), np.float32)).astype(ml_dtypes.bfloat16),
        iden_f=np.eye(128, dtype=np.float32),
        iota_row=np.arange(128, dtype=np.float32).reshape(1, 128),
        ones_row_f=np.ones((1, 128), np.float32),
        ones_row_bf=np.ones((1, 128), ml_dtypes.bfloat16),
        ones_col_bf=np.ones((128, 1), ml_dtypes.bfloat16),
    )
    in_maps = []
    for c in range(8):
        m = dict(shared)
        m["xT"] = np.ascontiguousarray(x[c].T)
        in_maps.append(m)
    return in_maps


def kernel(**inputs):
    in_maps = prep_inputs(**inputs)
    nc = build()
    res = run_bass_kernel_spmd(nc, in_maps, core_ids=list(range(8)))
    return np.stack([np.ascontiguousarray(r["yT"].T) for r in res.results], axis=0)


# version 09
# speedup vs baseline: 1.0901x; 1.0901x over previous
import math
import numpy as np
import ml_dtypes
import concourse.bass as bass
import concourse.mybir as mybir
from concourse.bass_utils import run_bass_kernel_spmd
from contextlib import ExitStack

F32 = mybir.dt.float32
BF16 = mybir.dt.bfloat16
I32 = mybir.dt.int32
AF = mybir.ActivationFunctionType
ALU = mybir.AluOpType
AX = mybir.AxisListType

D = 1024
T = 4096
NCH = 8
PASS = 512
NPASS = T // PASS
TCH = PASS // 128
NEXP = 8
EPS = 1e-5
PI = math.pi

SEMS = ("pe", "act", "dve", "d0", "d1", "wc", "ab", "wd", "x", "out", "cr", "wb")


class Ctx:
    def __init__(self, nc, stack):
        self.nc = nc
        self.stack = stack
        self.n = {s: 0 for s in SEMS}
        self.sem = {s: stack.enter_context(nc.semaphore("sem_" + s)) for s in SEMS}
        self.hw = {}
        self.wcache = {}

    def inc(self, ins, name, n=1):
        ins.then_inc(self.sem[name], n)
        self.n[name] += n
        return self.n[name]

    def wait(self, eng, label, name, val):
        if val is None or val <= 0:
            return
        key = (label, name)
        if self.hw.get(key, 0) >= val:
            return
        eng.wait_ge(self.sem[name], val)
        self.hw[key] = val


class Streams:
    def __init__(self, ctx):
        self.ctx = ctx
        self.items = {k: [] for k in ("pe", "act", "dve", "gp", "sp")}
        self.c = dict(ctx.n)
        self.ev = {}

    def mark(self, sem, key=None, n=1):
        self.c[sem] += n
        if key is not None:
            self.ev[key] = self.c[sem]
        return self.c[sem]

    def w(self, eng, sem, val):
        if val:
            self.items[eng].append(("w", sem, val))

    def op(self, eng, fn):
        self.items[eng].append(("c", fn))

    def run(self):
        ctx = self.ctx
        with ctx.nc.Block() as blk:
            def make(lbl, items):
                def body(eng):
                    for it in items:
                        if it[0] == "w":
                            ctx.wait(eng, lbl, it[1], it[2])
                        else:
                            it[1](eng)
                return body
            if self.items["pe"]:
                blk.tensor(make("pe", self.items["pe"]))
            if self.items["act"]:
                blk.scalar(make("act", self.items["act"]))
            if self.items["dve"]:
                blk.vector(make("dve", self.items["dve"]))
            if self.items["gp"]:
                blk.gpsimd(make("gp", self.items["gp"]))
            if self.items["sp"]:
                blk.sync(make("sp", self.items["sp"]))
        for s in SEMS:
            assert self.ctx.n[s] == self.c[s], (s, self.ctx.n[s], self.c[s])


def dma_cast_w(ctx, st, dram_2d, rows0, rows1, cols0, cols1, dst, dst_col0,
               sem, b16=None, wkey=None):
    ncols = cols1 - cols0
    nk = (rows1 - rows0) // 128
    ndma = 2 if nk > 4 else 1
    per = nk // ndma
    cached = wkey is not None and wkey in ctx.wcache
    if cached:
        st.w("gp", "wb", ctx.wcache[wkey])
    else:
        st.w("gp", "wb", st.c["wb"])
    wb_done = None
    for i in range(ndma):
        k0 = i * per
        kk = per
        sel = slice(rows0 + k0 * 128, rows0 + (k0 + kk) * 128)
        d = dst[:, dst_col0 + k0 * ncols: dst_col0 + (k0 + kk) * ncols]
        d = d.rearrange("p (k c) -> p k c", k=kk)
        if cached:
            src = b16[sel, cols0:cols1].rearrange("(k p) c -> p k c", p=128)
        else:
            src = dram_2d[sel, cols0:cols1].rearrange("(k p) c -> p k c", p=128)
        def fn(eng, s=src, d=d):
            ctx.inc(eng.dma_start(d, s), sem, 16)
        st.op("gp", fn)
        st.mark(sem, n=16)
        if b16 is not None and not cached:
            wdst = b16[sel, cols0:cols1].rearrange("(k p) c -> p k c", p=128)
            ld_ev = st.c[sem]
            st.w("sp", sem, ld_ev)
            def fn(eng, s=d, dd=wdst):
                ctx.inc(eng.dma_start(dd, s), "wb", 16)
            st.op("sp", fn)
            st.mark("wb", n=16)
            wb_done = st.c["wb"]
    if b16 is not None and not cached:
        ctx.wcache[wkey] = wb_done
    return st.c[sem]


def dma_flat(ctx, st, src2d, c0, c1, dst, dst0, sem, b16=None, wkey=None,
             ndma=1):
    ncols = c1 - c0
    cached = wkey is not None and wkey in ctx.wcache
    if cached:
        st.w("gp", "wb", ctx.wcache[wkey])
    else:
        st.w("gp", "wb", st.c["wb"])
    per = ncols // ndma
    wb_done = None
    for i in range(ndma):
        a0, a1 = c0 + i * per, c0 + (i + 1) * per
        d = dst[:, dst0 + i * per: dst0 + (i + 1) * per]
        s = (b16 if cached else src2d)[:, a0:a1]
        def fn(eng, s=s, d=d):
            ctx.inc(eng.dma_start(d, s), sem, 16)
        st.op("gp", fn)
        st.mark(sem, n=16)
        if b16 is not None and not cached:
            ld_ev = st.c[sem]
            st.w("sp", sem, ld_ev)
            def fn(eng, s=d, dd=b16[:, a0:a1]):
                ctx.inc(eng.dma_start(dd, s), "wb", 16)
            st.op("sp", fn)
            st.mark("wb", n=16)
            wb_done = st.c["wb"]
    if b16 is not None and not cached:
        ctx.wcache[wkey] = wb_done
    return st.c[sem]


def dma_x(ctx, st, dram_xT, p, xsb, store=False):
    for halfk in range(2):
        da = dram_xT[halfk * 512:(halfk + 1) * 512, p * PASS:(p + 1) * PASS]
        da = da.rearrange("(k p) c -> p k c", p=128)
        sl = xsb[:, halfk * 2048:(halfk + 1) * 2048].rearrange(
            "p (k c) -> p k c", k=4)
        if store:
            def fn(eng, a=da, b=sl):
                ctx.inc(eng.dma_start(a, b), "out", 16)
        else:
            def fn(eng, a=da, b=sl):
                ctx.inc(eng.dma_start(b, a), "x", 16)
        st.op("sp", fn)
        st.mark("out" if store else "x", n=16)
    return st.c["out" if store else "x"]


def emit_prologue(ctx, g, pst):
    nc = ctx.nc
    dr = g["dram"]
    sb = g
    sb["rowsf"] = pst.enter_context(nc.sbuf_tensor("rowsf", [1, 4096], F32))
    sb["prow"] = pst.enter_context(nc.sbuf_tensor("prow", [1, 3072], F32))
    sb["binrow"] = pst.enter_context(nc.sbuf_tensor("binrow", [1, 2048], F32))

    st = Streams(ctx)
    def ld(dst, src):
        def fn(eng, d=dst, s=src):
            ctx.inc(eng.dma_start(d, s), "cr", 16)
        st.op("sp", fn)
        st.mark("cr", n=16)
    ld(sb["tri"][:, :], dr["tri_bf"][:, :])
    ld(sb["iden"][:, :], dr["iden_f"][:, :])
    ld(sb["iota"][:, :], dr["iota_row"][:, :])
    ld(sb["ones1f"][:, :], dr["ones_row_f"][:, :])
    ld(sb["ones1b"][:, :], dr["ones_row_bf"][:, :])
    ld(sb["onescb"][:, :], dr["ones_col_bf"][:, :])
    ld(sb["biaspt"][:, :], dr["bias_pt"][:, :])
    ld(sb["gTs"][:, :].rearrange("p (k e) -> p k e", k=8),
       dr["gT"].rearrange("(k p) e -> p k e", p=128))
    def f0(eng):
        eng.memset(sb["c_eps"][:, 0:1], EPS)
    st.op("dve", f0)
    for e in range(3):
        ld(sb["prow"][0:1, e * D:(e + 1) * D], dr["plog"][e:e + 1, :])
    for e in range(2):
        ld(sb["binrow"][0:1, e * D:(e + 1) * D], dr["b_in"][e:e + 1, :])
    dma_x(ctx, st, dr["xT"], 0, sb["xs0"])
    st.run()
    cr0 = ctx.n["cr"]

    if True:
        pA = pst.enter_context(nc.psum_tensor("pro_a", [128, 512], F32))
        pB = pst.enter_context(nc.psum_tensor("pro_b", [128, 512], F32))
        pC = pst.enter_context(nc.psum_tensor("pro_c", [128, 512], F32))
        pD = pst.enter_context(nc.psum_tensor("pro_d", [128, 512], F32))
        tA = sb["xs1"][:, 0:1024]
        tB = sb["xs1"][:, 1024:2048]
        tC = sb["xs1"][:, 2048:3072]
        tD = sb["xs1"][:, 3072:4096]
        tI = pst.enter_context(nc.sbuf_tensor("proI", [128, 1024], F32))
        ti32 = pst.enter_context(nc.sbuf_tensor("proi32", [128, 1024], I32))

        rows = sb["rowsf"]
        nu_r = rows[0:1, 0:D]
        t2_r = rows[0:1, D:2 * D]
        gm_r = rows[0:1, 2 * D:3 * D]

        st = Streams(ctx)
        st.w("act", "cr", cr0)
        def f(s):
            s.activation(nu_r, sb["prow"][0:1, 0:D], AF.Exp)
            s.activation(t2_r, sb["prow"][0:1, D:2 * D], AF.Exp)
            s.mul(t2_r, t2_r, 1.0 / (2 * PI))
            s.activation(gm_r, sb["prow"][0:1, 2 * D:3 * D], AF.Exp)
            s.copy(sb["bbrow"][0:1, 0:D], sb["binrow"][0:1, 0:D])
            ctx.inc(s.copy(sb["bbrow"][0:1, D:2 * D], sb["binrow"][0:1, D:2 * D]),
                    "act")
        st.op("act", f)
        st.mark("act", "rows")
        st.run()

        def frac_reduce(v, t):
            for _ in range(2):
                v.tensor_copy(ti32[:, 0:1024], t[:, 0:1024])
                v.tensor_copy(tI[:, 0:1024], ti32[:, 0:1024])
                v.tensor_sub(t[:, 0:1024], t[:, 0:1024], tI[:, 0:1024])

        st = Streams(ctx)
        def f(t):
            for half in range(2):
                c0, c1 = half * 512, (half + 1) * 512
                t.matmul([pA, pB][half][:, :], sb["iota"][:, :], nu_r[:, c0:c1],
                         start=True, stop=True)
                ins = t.matmul([pC, pD][half][:, :], sb["iota"][:, :],
                               t2_r[:, c0:c1], start=True, stop=True)
            ctx.inc(ins, "pe")
        st.op("pe", f)
        st.mark("pe", "outer")
        st.w("dve", "pe", st.ev["outer"])
        def f(v):
            v.tensor_copy(tA[:, 0:512], pA[:, :])
            v.tensor_copy(tA[:, 512:1024], pB[:, :])
            v.tensor_copy(tB[:, 0:512], pC[:, :])
            v.tensor_copy(tB[:, 512:1024], pD[:, :])
            frac_reduce(v, tB)
            ctx.inc(v.tensor_copy(tI[:, 0:1], tB[:, 0:1]), "dve")
        st.op("dve", f)
        st.mark("dve", "frac")
        st.w("act", "dve", st.ev["frac"])
        def f(s):
            s.activation(tC[:, 0:1024], tB[:, 0:1024], AF.Sin, scale=2 * PI)
            s.activation(tD[:, 0:1024], tB[:, 0:1024], AF.Sin, scale=PI)
            s.activation(tD[:, 0:1024], tD[:, 0:1024], AF.Square)
            ctx.inc(s.activation(tA[:, 0:1024], tA[:, 0:1024], AF.Exp), "act")
        st.op("act", f)
        st.mark("act", "trig")
        st.run()

        st = Streams(ctx)
        def f(t):
            for half in range(2):
                ins = t.matmul([pA, pB][half][:, :], sb["ones1f"][:, :],
                               gm_r[:, half * 512:(half + 1) * 512],
                               start=True, stop=True)
            ctx.inc(ins, "pe")
        st.op("pe", f)
        st.mark("pe", "gb")
        st.w("dve", "pe", st.ev["gb"])
        def f(v):
            v.tensor_scalar(tD[:, 0:1024], tD[:, 0:1024], -2.0, 1.0,
                            ALU.mult, ALU.add)
            v.tensor_mul(tA[:, 0:512], tA[:, 0:512], pA[:, :])
            v.tensor_mul(tA[:, 512:1024], tA[:, 512:1024], pB[:, :])
            v.tensor_mul(sb["PGr"][:, :], tA[:, 0:1024], tD[:, 0:1024])
            v.tensor_mul(tI[:, 0:1024], tA[:, 0:1024], tC[:, 0:1024])
            ctx.inc(v.tensor_scalar_mul(sb["PGi"][:, :], tI[:, 0:1024], -1.0),
                    "dve")
        st.op("dve", f)
        st.mark("dve", "pg")
        st.run()

        st = Streams(ctx)
        def f(t):
            for m in range(8):
                q = [pA, pB][m // 4]
                t.matmul(q[:, (m % 4) * 128:(m % 4) * 128 + 128],
                         nu_r[:, m * 128:(m + 1) * 128], sb["iota"][:, :],
                         start=(m % 4 == 0), stop=True)
            for m in range(8):
                q = [pC, pD][m // 4]
                ins = t.matmul(q[:, (m % 4) * 128:(m % 4) * 128 + 128],
                               t2_r[:, m * 128:(m + 1) * 128], sb["iota"][:, :],
                               start=(m % 4 == 0), stop=True)
            ctx.inc(ins, "pe")
        st.op("pe", f)
        st.mark("pe", "outq")
        st.w("dve", "pe", st.ev["outq"])
        def f(v):
            v.tensor_copy(tB[:, 0:512], pC[:, :])
            v.tensor_copy(tB[:, 512:1024], pD[:, :])
            frac_reduce(v, tB)
            v.tensor_copy(tA[:, 0:512], pA[:, :])
            ctx.inc(v.tensor_copy(tA[:, 512:1024], pB[:, :]), "dve")
        st.op("dve", f)
        st.mark("dve", "fracq")
        st.w("act", "dve", st.ev["fracq"])
        def f(s):
            s.activation(tC[:, 0:1024], tB[:, 0:1024], AF.Sin, scale=2 * PI)
            s.activation(tD[:, 0:1024], tB[:, 0:1024], AF.Sin, scale=PI)
            s.activation(tD[:, 0:1024], tD[:, 0:1024], AF.Square)
            ctx.inc(s.activation(tA[:, 0:1024], tA[:, 0:1024], AF.Exp,
                                 scale=-1.0), "act")
        st.op("act", f)
        st.mark("act", "trigq")
        st.w("dve", "act", st.ev["trigq"])
        def f(v):
            v.tensor_scalar(tD[:, 0:1024], tD[:, 0:1024], -2.0, 1.0,
                            ALU.mult, ALU.add)
            v.tensor_mul(tB[:, 0:1024], tA[:, 0:1024], tD[:, 0:1024])
            v.tensor_mul(tC[:, 0:1024], tA[:, 0:1024], tC[:, 0:1024])
            v.tensor_copy(sb["Qr"][:, :], tB[:, 0:1024])
            v.tensor_copy(sb["Qi"][:, :], tC[:, 0:1024])
            v.tensor_scalar_mul(sb["Qineg"][:, :], tC[:, 0:1024], -1.0)
            s8 = sb["scr8"]
            v.tensor_copy(s8[:, 0:8], tB[:, 1:1024:128])
            v.tensor_copy(s8[:, 8:16], tC[:, 1:1024:128])
            v.tensor_copy(s8[:, 16:24], tB[:, 127:1024:128])
            v.tensor_copy(s8[:, 24:32], tC[:, 127:1024:128])
            bpt = sb["biaspt"]
            v.tensor_scalar_mul(sb["lcr"][:, 0:8], s8[:, 0:8], 1.0)
            v.tensor_mul(s8[:, 32:40], s8[:, 0:8], s8[:, 16:24])
            v.tensor_mul(s8[:, 40:48], s8[:, 8:16], s8[:, 24:32])
            v.tensor_add(sb["bo"][:, 0:8], bpt[:, 16:24], bpt[:, 24:32])
            v.tensor_sub(sb["QLr"][:, 0:8], s8[:, 32:40], s8[:, 40:48])
            v.tensor_mul(s8[:, 32:40], s8[:, 0:8], s8[:, 24:32])
            v.tensor_mul(s8[:, 40:48], s8[:, 8:16], s8[:, 16:24])
            v.tensor_sub(sb["bo"][:, 0:8], sb["bo"][:, 0:8], bpt[:, 32:40])
            v.tensor_add(sb["QLi"][:, 0:8], s8[:, 32:40], s8[:, 40:48])
            v.memset(sb["lcr"][:, 0:8], 0.0)
            ctx.inc(v.memset(sb["lci"][:, 0:8], 0.0), "dve")
        st.op("dve", f)
        st.mark("dve", "qtab")
        st.run()


def emit_ln(ctx, g, est, xs, xln, routed, ctile, pre=None, wait_x=None):
    nc = ctx.nc
    sb = g
    i = emit_ln.i
    emit_ln.i += 1
    scr = {}
    scr["rows"] = est.enter_context(nc.sbuf_tensor(f"lnr{i}", [1, 2048], F32))
    scr["sqb"] = est.enter_context(nc.sbuf_tensor(f"sqb{i}", [128, 1024], BF16))
    scr["xbb"] = est.enter_context(nc.sbuf_tensor(f"xbb{i}", [128, 1024], BF16))
    scr["zf"] = est.enter_context(nc.sbuf_tensor(f"zf{i}", [128, 2048], F32))
    with ExitStack() as pst:
        ps_sum = pst.enter_context(nc.psum_tensor(f"pss{i}", [128, 512], F32))
        ps_sq = pst.enter_context(nc.psum_tensor(f"psq{i}", [128, 512], F32))
        ps_A = pst.enter_context(nc.psum_tensor(f"psA{i}", [128, 512], F32))
        ps_B = pst.enter_context(nc.psum_tensor(f"psB{i}", [128, 512], F32))
        ps_log = None
        if routed:
            ps_log = [pst.enter_context(nc.psum_tensor(f"psl{i}_{s}", [128, 512],
                                                       F32)) for s in range(4)]
            for nme, shp in (("sc8", 32), ("g4", 32), ("msk", 32), ("comb", 32),
                             ("mx", 4), ("nmx", 4), ("m8", 64), ("gm4", 16)):
                scr[nme] = est.enter_context(
                    nc.sbuf_tensor(f"{nme}{i}", [128, shp], F32))
            scr["cTsb"] = est.enter_context(nc.sbuf_tensor(f"cT{i}", [8, 512], F32))

        st = Streams(ctx)
        if pre:
            pre(st)
        rows = scr["rows"]
        m_r, e_r = rows[0:1, 0:512], rows[0:1, 512:1024]
        A_r, B_r = rows[0:1, 1024:1536], rows[0:1, 1536:2048]
        sqb, xbb, zf = scr["sqb"], scr["xbb"], scr["zf"]
        nz = 3

        for k in range(NCH):
            sl = k % 2
            if k >= 2:
                st.w("act", "pe", st.ev[("st", k - 2)])
            if k == 0 and wait_x is not None:
                st.w("act", "x", wait_x)
            def f(s, k=k, sl=sl):
                s.activation(sqb[:, sl * 512:(sl + 1) * 512],
                             xs[:, k * 512:(k + 1) * 512], AF.Square)
                ctx.inc(s.copy(xbb[:, sl * 512:(sl + 1) * 512],
                               xs[:, k * 512:(k + 1) * 512]), "act")
            st.op("act", f)
            st.mark("act", ("sq", k))
            st.w("pe", "act", st.ev[("sq", k)])
            def f(t, k=k, sl=sl):
                t.matmul(ps_sum[0:1, :], sb["onescb"][:, :],
                         xbb[:, sl * 512:(sl + 1) * 512],
                         start=(k == 0), stop=(k == NCH - 1))
                ctx.inc(t.matmul(ps_sq[0:1, :], sb["onescb"][:, :],
                                 sqb[:, sl * 512:(sl + 1) * 512],
                                 start=(k == 0), stop=(k == NCH - 1)), "pe")
            st.op("pe", f)
            st.mark("pe", ("st", k))

        st.w("act", "pe", st.ev[("st", NCH - 1)])
        def f(s):
            s.mul(m_r, ps_sum[0:1, :], 1.0 / D)
            ctx.inc(s.mul(e_r, ps_sq[0:1, :], 1.0 / D), "act")
        st.op("act", f)
        st.mark("act", "me")
        st.w("dve", "act", st.ev["me"])
        def f(v):
            v.tensor_mul(B_r, m_r, m_r)
            ctx.inc(v.tensor_sub(e_r, e_r, B_r), "dve")
        st.op("dve", f)
        st.mark("dve", "var")
        st.w("act", "dve", st.ev["var"])
        def f(s):
            s.activation(A_r, e_r, AF.Ln, bias=EPS)
            ctx.inc(s.activation(A_r, A_r, AF.Exp, scale=-0.5), "act")
        st.op("act", f)
        st.mark("act", "A")
        st.w("dve", "act", st.ev["A"])
        def f(v):
            ctx.inc(v.scalar_tensor_tensor(B_r, m_r, -1.0, A_r, ALU.mult,
                                           ALU.mult), "dve")
        st.op("dve", f)
        st.mark("dve", "B")
        st.w("pe", "dve", st.ev["B"])
        def f(t):
            t.matmul(ps_A[:, :], sb["ones1f"][:, :], A_r, start=True, stop=True)
            ctx.inc(t.matmul(ps_B[:, :], sb["ones1f"][:, :], B_r, start=True,
                             stop=True), "pe")
        st.op("pe", f)
        st.mark("pe", "bc")

        psA3 = ps_A[:, :].rearrange("p (o c) -> p o c", o=1).broadcast_to(
            [128, 4, 512])
        psB3 = ps_B[:, :].rearrange("p (o c) -> p o c", o=1).broadcast_to(
            [128, 4, 512])
        for hf in range(2):
            x3 = xs[:, hf * 2048:(hf + 1) * 2048].rearrange(
                "p (k c) -> p k c", k=4)
            st.w("dve", "pe", st.ev["bc"])
            if hf == 1:
                st.w("dve", "act", st.ev[("xl", 0)])
                if routed:
                    st.w("dve", "pe", st.ev[("gt", 0)])
            z3 = zf[:, 0:2048].rearrange("p (k c) -> p k c", k=4)
            if routed:
                def f(v, x3=x3, z3=z3):
                    v.tensor_tensor(z3, x3, psA3, ALU.mult)
                    ctx.inc(v.tensor_tensor(z3, z3, psB3, ALU.add), "dve")
                st.op("dve", f)
                st.mark("dve", ("z", hf))
                st.w("act", "dve", st.ev[("z", hf)])
                def f(s, hf=hf):
                    ctx.inc(s.copy(xln[:, hf * 2048:(hf + 1) * 2048],
                                   zf[:, 0:2048]), "act")
                st.op("act", f)
                st.mark("act", ("xl", hf))
                st.w("pe", "dve", st.ev[("z", hf)])
                def f(t, hf=hf):
                    for k in range(4):
                        kk = hf * 4 + k
                        for sub in range(4):
                            ins = t.matmul(ps_log[sub][:, 0:8],
                                           zf[:, k * 512 + sub * 128:
                                              k * 512 + (sub + 1) * 128],
                                           sb["gTs"][:, kk * 8:(kk + 1) * 8],
                                           start=(kk == 0),
                                           stop=(kk == NCH - 1))
                    ctx.inc(ins, "pe")
                st.op("pe", f)
                st.mark("pe", ("gt", hf))
            else:
                xl3 = xln[:, hf * 2048:(hf + 1) * 2048].rearrange(
                    "p (k c) -> p k c", k=4)
                def f(v, x3=x3, z3=z3, xl3=xl3):
                    v.tensor_tensor(z3, x3, psA3, ALU.mult)
                    ctx.inc(v.tensor_tensor(xl3, z3, psB3, ALU.add), "dve")
                st.op("dve", f)
                st.mark("dve", ("z", hf))
                st.ev[("xl", hf)] = 0

        if routed:
            sc8, g4, msk = scr["sc8"], scr["g4"], scr["msk"]
            comb, mx, nmx, m8 = scr["comb"], scr["mx"], scr["nmx"], scr["m8"]
            gm4 = scr["gm4"]
            ps_cT = ps_sum
            st.w("dve", "pe", st.ev[("gt", 1)])
            def f(v):
                for sub in range(4):
                    v.reduce_max(mx[:, sub:sub + 1],
                                 ps_log[sub][:, 0:8], AX.X)
            st.op("dve", f)
            for sub in range(4):
                def f(v, sub=sub):
                    ctx.inc(v.tensor_scalar_mul(nmx[:, sub:sub + 1],
                                                mx[:, sub:sub + 1], -1.0),
                            "dve")
                st.op("dve", f)
                st.mark("dve", ("nm", sub))
            for sub in range(4):
                st.w("act", "dve", st.ev[("nm", sub)])
                def f(s, sub=sub):
                    ctx.inc(s.activation(sc8[:, sub * 8:sub * 8 + 8],
                                         ps_log[sub][:, 0:8], AF.Exp,
                                         bias=nmx[:, sub:sub + 1]), "act")
                st.op("act", f)
                st.mark("act", ("ex", sub))
            st.w("dve", "act", st.ev[("ex", 3)])
            def f(v):
                for sub in range(4):
                    v.reduce_sum(mx[:, sub:sub + 1], sc8[:, sub * 8:sub * 8 + 8],
                                 AX.X)
                for sub in range(4):
                    v.reciprocal(nmx[:, sub:sub + 1], mx[:, sub:sub + 1])
                for sub in range(4):
                    v.tensor_scalar_mul(sc8[:, sub * 8:sub * 8 + 8],
                                        sc8[:, sub * 8:sub * 8 + 8],
                                        nmx[:, sub:sub + 1])
                for sub in range(4):
                    v.memset(g4[:, sub * 8 + 4:sub * 8 + 8], 0.0)
                for sub in range(4):
                    s8 = sc8[:, sub * 8:sub * 8 + 8]
                    v.tensor_tensor(g4[:, sub * 8:sub * 8 + 4], s8[:, 0:8:2],
                                    s8[:, 1:8:2], ALU.max)
                for sub in range(4):
                    v.max(out=m8[:, sub * 16:sub * 16 + 8],
                          in_=g4[:, sub * 8:sub * 8 + 8])
                for sub in range(4):
                    v.tensor_scalar(gm4[:, sub * 4:sub * 4 + 4],
                                    g4[:, sub * 8:sub * 8 + 4],
                                    m8[:, sub * 16 + 1:sub * 16 + 2], None,
                                    ALU.is_ge)
                for sub in range(4):
                    mk = msk[:, sub * 8:sub * 8 + 8]
                    v.tensor_copy(mk[:, 0:8:2], gm4[:, sub * 4:sub * 4 + 4])
                for sub in range(4):
                    mk = msk[:, sub * 8:sub * 8 + 8]
                    v.tensor_copy(mk[:, 1:8:2], gm4[:, sub * 4:sub * 4 + 4])
                for sub in range(4):
                    mk = msk[:, sub * 8:sub * 8 + 8]
                    v.tensor_mul(mk, mk, sc8[:, sub * 8:sub * 8 + 8])
                for sub in range(4):
                    v.max(out=m8[:, sub * 16 + 8:sub * 16 + 16],
                          in_=msk[:, sub * 8:sub * 8 + 8])
                for sub in range(4):
                    v.tensor_add(mx[:, sub:sub + 1],
                                 m8[:, sub * 16 + 8:sub * 16 + 9],
                                 m8[:, sub * 16 + 9:sub * 16 + 10])
                for sub in range(4):
                    v.reciprocal(nmx[:, sub:sub + 1], mx[:, sub:sub + 1])
                for sub in range(4):
                    v.tensor_scalar(comb[:, sub * 8:sub * 8 + 8],
                                    msk[:, sub * 8:sub * 8 + 8],
                                    m8[:, sub * 16 + 9:sub * 16 + 10], None,
                                    ALU.is_ge)
                for sub in range(4):
                    v.tensor_mul(comb[:, sub * 8:sub * 8 + 8],
                                 comb[:, sub * 8:sub * 8 + 8],
                                 msk[:, sub * 8:sub * 8 + 8])
            st.op("dve", f)
            for sub in range(4):
                def f(v, sub=sub):
                    ctx.inc(v.tensor_scalar(comb[:, sub * 8:sub * 8 + 8],
                                            comb[:, sub * 8:sub * 8 + 8],
                                            nmx[:, sub:sub + 1], 0.5,
                                            ALU.mult, ALU.mult), "dve")
                st.op("dve", f)
                st.mark("dve", ("cm", sub))
            for sub in range(4):
                st.w("pe", "dve", st.ev[("cm", sub)])
                def f(t, sub=sub):
                    ctx.inc(t.matmul(ps_cT[0:8, sub * 128:(sub + 1) * 128],
                                     comb[:, sub * 8:sub * 8 + 8],
                                     sb["iden"][:, :], start=(sub == 0),
                                     stop=True), "pe")
                st.op("pe", f)
                st.mark("pe", ("cT", sub))
            st.w("pe", "dve", st.ev[("cm", 3)])
            def f(t):
                ctx.inc(t.matmul(ps_sq[0:8, 0:128], comb[:, 0:8],
                                 sb["iden"][:, :], start=True, stop=True), "pe")
            st.op("pe", f)
            st.mark("pe", "cTguard")
            st.w("dve", "pe", st.ev["cTguard"])
            def f(v):
                ctx.inc(v.tensor_copy(scr["cTsb"][0:8, 0:512],
                                      ps_cT[0:8, 0:512]), "dve")
            st.op("dve", f)
            st.mark("dve", "cte")
            st.w("gp", "dve", st.ev["cte"])
            for e in range(NEXP):
                def f(gp, e=e):
                    ctx.inc(gp.dma_start(
                        sb["crows"][0:1, e * 1024 + ctile * 512:
                                    e * 1024 + (ctile + 1) * 512],
                        scr["cTsb"][e:e + 1, 0:512]), "cr", 16)
                st.op("gp", f)
                st.mark("cr", n=16)
        st.run()
emit_ln.i = 0


def emit_ffn_exp(ctx, g, est, tiles, pre_e, w2_0, post=None, mirror=True):
    nc = ctx.nc
    sb = g
    dr = g["dram"]
    i = emit_ffn_exp.i
    emit_ffn_exp.i += 1
    nt = len(tiles)
    act_bufs = [est.enter_context(nc.sbuf_tensor(f"fac{i}_{t}", [128, 8192], BF16))
                for t in range(nt)]
    silu_sb = est.enter_context(nc.sbuf_tensor(f"fsl{i}", [128, 1024], F32))
    tmp2 = est.enter_context(nc.sbuf_tensor(f"ftm{i}", [128, 1024], F32))
    csb = est.enter_context(nc.sbuf_tensor(f"fcs{i}", [128, 1024], F32))
    with ExitStack() as pst:
        pp = pst.enter_context(nc.psum_tensor(f"fpp{i}", [128, 2048], F32))
        po = pst.enter_context(nc.psum_tensor(f"fpo{i}", [128, 2048], F32))

        st = Streams(ctx)
        wD, wC = sb["wD"], sb["wC"]
        crow_ev = ctx.n["cr"]

        sim = {k: ctx.n[k] for k in ("d0", "d1", "wc")}
        wait_eighth = dict(pre_e)
        wait_w2 = {0: w2_0}
        gp_plan = []
        for e in range(NEXP):
            for q in range(8):
                if (e, q) in wait_eighth:
                    continue
                sim["d%d" % (q % 2)] += 32
                wait_eighth[(e, q)] = sim["d%d" % (q % 2)]
                gp_plan.append(("e8", e, q))
            if e + 1 < NEXP:
                sim["wc"] += 64
                wait_w2[e + 1] = sim["wc"]
                gp_plan.append(("w2", e + 1))

        last = {}
        punit = {}
        pu_ctr = [0]

        def pe_cb(e):
            for ti in range(nt):
                bank = ti
                st.w("pe", "cr", crow_ev)
                if ("po", bank) in last:
                    st.w("pe", "dve", last[("po", bank)])
                def f(t, e=e, ti=ti, bank=bank):
                    ctx.inc(t.matmul(
                        po[:, bank * 512:(bank + 1) * 512], sb["ones1b"][:, :],
                        sb["crows"][0:1, e * 1024 + tiles[ti][2] * 512:
                                    e * 1024 + (tiles[ti][2] + 1) * 512],
                        start=True, stop=True), "pe")
                st.op("pe", f)
                st.mark("pe", ("cb", e, ti))
                st.w("dve", "pe", st.ev[("cb", e, ti)])
                def f(v, ti=ti, bank=bank):
                    ctx.inc(v.tensor_copy(csb[:, ti * 512:(ti + 1) * 512],
                                          po[:, bank * 512:(bank + 1) * 512]),
                            "dve")
                st.op("dve", f)
                st.mark("dve", ("cbe", e, ti))
                last[("po", bank)] = st.ev[("cbe", e, ti)]

        def pe_pairs(e):
            for m in range(16):
                q = m // 2
                for ti in range(nt):
                    pu = pu_ctr[0]
                    sslot = pu % 2
                    st.w("pe", "d%d" % (q % 2), wait_eighth[(e, q)])
                    if ("pp", sslot) in last:
                        st.w("pe", "dve", last[("pp", sslot)])
                    buf = (q % 2) * 4096
                    gbase = (m % 2) * 2048
                    xl = tiles[ti][1]
                    def f(t, buf=buf, gbase=gbase, xl=xl, sslot=sslot):
                        gcol = sslot * 1024
                        for k in range(8):
                            t.matmul(pp[:, gcol:gcol + 512],
                                     wD[:, buf + gbase + k * 128:
                                        buf + gbase + (k + 1) * 128],
                                     xl[:, k * 512:(k + 1) * 512],
                                     start=(k == 0), stop=(k == 7))
                        for k in range(8):
                            ins = t.matmul(pp[:, gcol + 512:gcol + 1024],
                                           wD[:, buf + gbase + 1024 + k * 128:
                                              buf + gbase + 1024 + (k + 1) * 128],
                                           xl[:, k * 512:(k + 1) * 512],
                                           start=(k == 0), stop=(k == 7))
                        ctx.inc(ins, "pe")
                    st.op("pe", f)
                    st.mark("pe", ("pr", pu))
                    punit[(e, m, ti)] = pu
                    st.w("dve", "pe", st.ev[("pr", pu)])
                    if e >= 1:
                        st.w("dve", "pe", st.ev[("o", e - 1, ti, 7)])
                    def f(v, sslot=sslot):
                        ctx.inc(v.tensor_copy(
                            silu_sb[:, sslot * 512:(sslot + 1) * 512],
                            pp[:, sslot * 1024:sslot * 1024 + 512]), "dve")
                    st.op("dve", f)
                    st.mark("dve", ("pc", pu))
                    st.w("act", "dve", st.ev[("pc", pu)])
                    if pu >= 2:
                        st.w("act", "dve", st.ev[("pa", pu - 2)])
                    def f(s, sslot=sslot):
                        ctx.inc(s.activation(
                            tmp2[:, sslot * 512:(sslot + 1) * 512],
                            silu_sb[:, sslot * 512:(sslot + 1) * 512], AF.Silu),
                            "act")
                    st.op("act", f)
                    st.mark("act", ("si", pu))
                    st.w("dve", "act", st.ev[("si", pu)])
                    def f(v, e=e, m=m, ti=ti, sslot=sslot):
                        v.tensor_mul(silu_sb[:, sslot * 512:(sslot + 1) * 512],
                                     tmp2[:, sslot * 512:(sslot + 1) * 512],
                                     csb[:, ti * 512:(ti + 1) * 512])
                        ctx.inc(v.tensor_mul(
                            act_bufs[ti][:, m * 512:(m + 1) * 512],
                            silu_sb[:, sslot * 512:(sslot + 1) * 512],
                            pp[:, sslot * 1024 + 512:sslot * 1024 + 1024]),
                            "dve")
                    st.op("dve", f)
                    st.mark("dve", ("pa", pu))
                    last[("pp", sslot)] = st.ev[("pa", pu)]
                    pu_ctr[0] += 1

        def pe_mm2(e):
            for ti in range(nt):
                xs = tiles[ti][0]
                for h in range(8):
                    bank = h % 4
                    st.w("pe", "wc", wait_w2[e])
                    st.w("pe", "dve", st.ev[("pa", punit[(e, 15, nt - 1)])])
                    if ("po", bank) in last:
                        st.w("pe", "dve", last[("po", bank)])
                    def f(t, ti=ti, h=h, bank=bank):
                        for mm in range(16):
                            ins = t.matmul(
                                po[:, bank * 512:(bank + 1) * 512],
                                wC[:, h * 2048 + mm * 128: h * 2048 + (mm + 1) * 128],
                                act_bufs[ti][:, mm * 512:(mm + 1) * 512],
                                start=(mm == 0), stop=(mm == 15))
                        ctx.inc(ins, "pe")
                    st.op("pe", f)
                    st.mark("pe", ("o", e, ti, h))
                    st.w("dve", "pe", st.ev[("o", e, ti, h)])
                    def f(v, ti=ti, h=h, bank=bank, xs=xs):
                        ctx.inc(v.tensor_add(xs[:, h * 512:(h + 1) * 512],
                                             xs[:, h * 512:(h + 1) * 512],
                                             po[:, bank * 512:(bank + 1) * 512]),
                                "dve")
                    st.op("dve", f)
                    st.mark("dve", ("xa", e, ti, h))
                    last[("po", bank)] = st.ev[("xa", e, ti, h)]

        for e in range(NEXP):
            pe_cb(e)
            pe_pairs(e)
            pe_mm2(e)

        for item in gp_plan:
            if item[0] == "e8":
                _, e, q = item
                pe_, pq_ = (e, q - 2) if q >= 2 else (e - 1, q + 6)
                kk = (pe_, 2 * pq_ + 1, nt - 1)
                if kk in punit:
                    st.w("gp", "pe", st.ev[("pr", punit[kk])])
                v = dma_flat(ctx, st, dr["w13"][e][q], 0, 4096, wD,
                             (q % 2) * 4096, "d%d" % (q % 2),
                             b16=dr["w13b"][e][q] if mirror else None,
                             wkey=("w13", e, q) if mirror else None, ndma=2)
                assert v == wait_eighth[(e, q)], (e, q, v, wait_eighth[(e, q)])
            else:
                _, e = item
                st.w("gp", "pe", st.ev[("o", e - 1, nt - 1, 7)])
                for j in range(4):
                    v = dma_flat(ctx, st, dr["w2t"][e], j * 4096, (j + 1) * 4096,
                                 wC, j * 4096, "wc",
                                 b16=dr["w2b"][e] if mirror else None,
                                 wkey=("w2", e, j) if mirror else None)
                assert v == wait_w2[e], (e, v, wait_w2[e])
        if post:
            post(st)
        st.run()
        return st.ev[("xa", NEXP - 1, 0, 7)]
emit_ffn_exp.i = 0


def attn_scratch(ctx, est, xl_nxt, xl_cur):
    nc = ctx.nc
    i = attn_scratch.i
    attn_scratch.i += 1
    scr = {"i": i}
    for nme, shape, dt in (("out_fm", [128, 4096], F32),
                           ("ytmp", [128, 2048], F32)):
        scr[nme] = est.enter_context(nc.sbuf_tensor(f"a{nme}{i}", shape, dt))
    scr["out_bf"] = xl_nxt
    scr["y1b"] = xl_cur
    return scr
attn_scratch.i = 0


def emit_attn_A(ctx, g, scr, xln, win_ab, win_wd):
    nc = ctx.nc
    sb = g
    i = scr["i"]
    with ExitStack() as pst:
        pu_ps = pst.enter_context(nc.psum_tensor(f"apu{i}", [128, 2048], F32))
        pS = pst.enter_context(nc.psum_tensor(f"apS{i}", [128, 2048], F32))
        vbf = pst.enter_context(nc.sbuf_tensor(f"avbf{i}", [128, 2048], BF16))
        vt = scr["out_fm"][:, 0:2048]
        st = Streams(ctx)
        hr, hin = scr["hr"], scr["hin"]
        wAB, wD = sb["wAB"], sb["wD"]
        for c in range(TCH):
            st.w("pe", "ab", win_ab)
            st.w("pe", "wd", win_wd)
            if c >= 1:
                st.w("pe", "dve", st.ev[("v", c - 1)])
            def f(t, c=c):
                for e2 in range(2):
                    wbuf = wAB if e2 == 0 else wD
                    for half in range(2):
                        bank = e2 * 2 + half
                        dst = pu_ps[:, bank * 512:(bank + 1) * 512]
                        for k in range(8):
                            t.matmul(dst,
                                     xln[:, k * 512 + c * 128:
                                         k * 512 + (c + 1) * 128],
                                     wbuf[:, k * 1024 + half * 512:
                                          k * 1024 + (half + 1) * 512],
                                     start=(k == 0), stop=False)
                        ins = t.matmul(dst, sb["ones1b"][:, :],
                                       sb["bbrow"][0:1, e2 * D + half * 512:
                                                   e2 * D + (half + 1) * 512],
                                       start=False, stop=True)
                ctx.inc(ins, "pe")
            st.op("pe", f)
            st.mark("pe", ("ug", c))
            st.w("dve", "pe", st.ev[("ug", c)])
            if c >= 1:
                st.w("dve", "pe", st.ev[("S", c - 1)])
            def f(v, c=c):
                for half in range(2):
                    h0 = half * 512
                    v.tensor_mul(vt[:, 0:512], sb["PGr"][:, h0:h0 + 512],
                                 pu_ps[:, h0:h0 + 512])
                    v.tensor_mul(vt[:, 512:1024], sb["PGi"][:, h0:h0 + 512],
                                 pu_ps[:, 1024 + h0:1024 + h0 + 512])
                    v.tensor_sub(vbf[:, h0:h0 + 512], vt[:, 0:512],
                                 vt[:, 512:1024])
                    v.tensor_mul(vt[:, 0:512], sb["PGi"][:, h0:h0 + 512],
                                 pu_ps[:, h0:h0 + 512])
                    v.tensor_mul(vt[:, 512:1024], sb["PGr"][:, h0:h0 + 512],
                                 pu_ps[:, 1024 + h0:1024 + h0 + 512])
                    ins = v.tensor_add(vbf[:, 1024 + h0:1024 + h0 + 512],
                                       vt[:, 0:512], vt[:, 512:1024])
                ctx.inc(ins, "dve")
            st.op("dve", f)
            st.mark("dve", ("v", c))
            st.w("pe", "dve", st.ev[("v", c)])
            if c >= 1:
                st.w("pe", "dve", st.ev[("lc", c - 1)])
            def f(t, c=c):
                for comp in range(2):
                    for m in range(8):
                        bank = comp * 2 + m // 4
                        c0 = bank * 512 + (m % 4) * 128
                        ins = t.matmul(pS[:, c0:c0 + 128],
                                       vbf[:, comp * 1024 + m * 128:
                                           comp * 1024 + (m + 1) * 128],
                                       sb["tri"][:, :], start=(m % 4 == 0),
                                       stop=True)
                ctx.inc(ins, "pe")
            st.op("pe", f)
            st.mark("pe", ("S", c))
            st.w("dve", "pe", st.ev[("S", c)])
            hr3 = hr[:, :].rearrange("p (m u) -> p m u", m=8)[
                :, :, c * 128:(c + 1) * 128]
            hi3 = hin[:, :].rearrange("p (m u) -> p m u", m=8)[
                :, :, c * 128:(c + 1) * 128]
            def f(v, c=c, hr3=hr3, hi3=hi3):
                s8 = sb["scr8"]
                pr3 = pS[:, 0:1024].rearrange("p (m t) -> p m t", m=8)
                pi3 = pS[:, 1024:2048].rearrange("p (m t) -> p m t", m=8)
                lc3r = sb["lcr"][:, 0:8].rearrange(
                    "p (m o) -> p m o", o=1).broadcast_to([128, 8, 128])
                lc3i = sb["lci"][:, 0:8].rearrange(
                    "p (m o) -> p m o", o=1).broadcast_to([128, 8, 128])
                vt0 = vt[:, 0:1024].rearrange("p (m t) -> p m t", m=8)
                vt1 = vt[:, 1024:2048].rearrange("p (m t) -> p m t", m=8)
                v.tensor_tensor(pr3, pr3, lc3r, ALU.add)
                v.tensor_tensor(pi3, pi3, lc3i, ALU.add)
                v.tensor_mul(vt[:, 0:1024], pS[:, 0:1024], sb["Qr"][:, :])
                v.tensor_mul(vt[:, 1024:2048], pS[:, 1024:2048], sb["Qi"][:, :])
                v.tensor_copy(s8[:, 32:40], pS[:, 127:1024:128])
                v.tensor_copy(s8[:, 40:48], pS[:, 1151:2048:128])
                v.tensor_sub(hr3, vt0, vt1)
                v.tensor_mul(vt[:, 0:1024], pS[:, 0:1024], sb["Qineg"][:, :])
                v.tensor_mul(vt[:, 1024:2048], pS[:, 1024:2048], sb["Qr"][:, :])
                ins = v.tensor_sub(hi3, vt0, vt1)
                ctx.inc(ins, "dve")
            st.op("dve", f)
            st.mark("dve", ("h", c))
            def f(v, c=c):
                s8 = sb["scr8"]
                v.tensor_mul(s8[:, 0:8], sb["QLr"][:, 0:8], s8[:, 32:40])
                v.tensor_mul(s8[:, 8:16], sb["QLi"][:, 0:8], s8[:, 40:48])
                v.tensor_mul(s8[:, 16:24], sb["QLr"][:, 0:8], s8[:, 40:48])
                v.tensor_mul(s8[:, 24:32], sb["QLi"][:, 0:8], s8[:, 32:40])
                v.tensor_sub(sb["lcr"][:, 0:8], s8[:, 0:8], s8[:, 8:16])
                ctx.inc(v.tensor_add(sb["lci"][:, 0:8], s8[:, 16:24],
                                     s8[:, 24:32]), "dve")
            st.op("dve", f)
            st.mark("dve", ("lc", c))
        st.run()


def emit_attn_B1(ctx, g, scr, xln, wmid_ev, pre=None):
    nc = ctx.nc
    sb = g
    with ExitStack() as pst:
        po2 = pst.enter_context(nc.psum_tensor(f"apo{scr['i']}",
                                               [128, 1024], F32))
        st = Streams(ctx)
        if pre:
            pre(st)
        win2_ev = scr["win2_ev"]
        hr, hin = scr["hr"], scr["hin"]
        out_fm, out_bf = scr["out_fm"], scr["out_bf"]
        wC, wD = sb["wC"], sb["wE"]
        for j in range(8):
            bank = j % 2
            st.w("pe", "wc", wmid_ev)
            st.w("pe", "wd", win2_ev)
            if j >= 2:
                st.w("pe", "act", st.ev[("oe", j - 2)])
            def f(t, j=j, bank=bank):
                dst = po2[:, bank * 512:(bank + 1) * 512]
                for k in range(8):
                    t.matmul(dst, wC[:, j * 1024 + k * 128: j * 1024 + (k + 1) * 128],
                             hr[:, k * 512:(k + 1) * 512], start=(k == 0),
                             stop=False)
                for k in range(8):
                    t.matmul(dst, wC[:, 8192 + j * 1024 + k * 128:
                                    8192 + j * 1024 + (k + 1) * 128],
                             hin[:, k * 512:(k + 1) * 512], start=False,
                             stop=False)
                for k in range(8):
                    ins = t.matmul(dst,
                                   wD[:, j * 1024 + k * 128: j * 1024 + (k + 1) * 128],
                                   xln[:, k * 512:(k + 1) * 512], start=False,
                                   stop=(k == 7))
                ctx.inc(ins, "pe")
            st.op("pe", f)
            st.mark("pe", ("og", j))
            st.w("dve", "pe", st.ev[("og", j)])
            def f(v, j=j, bank=bank):
                ctx.inc(v.tensor_scalar_add(out_fm[:, j * 512:(j + 1) * 512],
                                            po2[:, bank * 512:(bank + 1) * 512],
                                            sb["bo"][:, j:j + 1]), "dve")
            st.op("dve", f)
            st.mark("dve", ("od", j))
            st.w("act", "dve", st.ev[("od", j)])
            def f(s, j=j):
                ctx.inc(s.copy(out_bf[:, j * 512:(j + 1) * 512],
                               out_fm[:, j * 512:(j + 1) * 512]), "act")
            st.op("act", f)
            st.mark("act", ("oe", j))
        st.run()


def emit_attn_B2(ctx, g, scr, xs, w1_ev, pre=None):
    nc = ctx.nc
    sb = g
    i = scr["i"]
    bpt = sb["biaspt"]
    with ExitStack() as pst:
        py = pst.enter_context(nc.psum_tensor(f"apy{i}", [128, 1024], F32))
        st = Streams(ctx)
        if pre:
            pre(st)
        w2_ev = scr["w2w_ev"]
        out_fm, out_bf, y1b = scr["out_fm"], scr["out_bf"], scr["y1b"]
        ytmp = scr["ytmp"]
        wAB, wC = sb["wAB"], sb["wC"]
        for j in range(8):
            bank = j % 2
            st.w("pe", "ab", w1_ev)
            if j >= 2:
                st.w("pe", "act", st.ev[("g", j - 2)])
            def f(t, j=j, bank=bank):
                dst = py[:, bank * 512:(bank + 1) * 512]
                for k in range(8):
                    ins = t.matmul(dst,
                                   wAB[:, j * 1024 + k * 128: j * 1024 + (k + 1) * 128],
                                   out_bf[:, k * 512:(k + 1) * 512],
                                   start=(k == 0), stop=(k == 7))
                ctx.inc(ins, "pe")
            st.op("pe", f)
            st.mark("pe", ("y1", j))
            st.w("dve", "pe", st.ev[("y1", j)])
            def f(v, j=j, bank=bank):
                ctx.inc(v.tensor_scalar_add(ytmp[:, (j % 2) * 512:(j % 2 + 1) * 512],
                                            py[:, bank * 512:(bank + 1) * 512],
                                            bpt[:, 40 + j:40 + j + 1]), "dve")
            st.op("dve", f)
            st.mark("dve", ("y1d", j))
            st.w("act", "dve", st.ev[("y1d", j)])
            def f(s, j=j):
                ctx.inc(s.activation(y1b[:, j * 512:(j + 1) * 512],
                                     ytmp[:, (j % 2) * 512:(j % 2 + 1) * 512],
                                     AF.Gelu), "act")
            st.op("act", f)
            st.mark("act", ("g", j))
        for j in range(8):
            bank = j % 2
            st.w("pe", "wc", w2_ev)
            st.w("pe", "act", st.ev[("g", 7)])
            if j >= 2:
                st.w("pe", "dve", st.ev[("ye", j - 2)])
            def f(t, j=j, bank=bank):
                dst = py[:, bank * 512:(bank + 1) * 512]
                for k in range(8):
                    ins = t.matmul(dst,
                                   wC[:, j * 1024 + k * 128: j * 1024 + (k + 1) * 128],
                                   y1b[:, k * 512:(k + 1) * 512],
                                   start=(k == 0), stop=(k == 7))
                ctx.inc(ins, "pe")
            st.op("pe", f)
            st.mark("pe", ("y2", j))
            st.w("dve", "pe", st.ev[("y2", j)])
            def f(v, j=j, bank=bank):
                v.tensor_scalar_add(ytmp[:, (j % 2) * 512:(j % 2 + 1) * 512],
                                    py[:, bank * 512:(bank + 1) * 512],
                                    bpt[:, 48 + j:48 + j + 1])
                ctx.inc(v.tensor_add(out_fm[:, j * 512:(j + 1) * 512],
                                     out_fm[:, j * 512:(j + 1) * 512],
                                     ytmp[:, (j % 2) * 512:(j % 2 + 1) * 512]),
                        "dve")
            st.op("dve", f)
            st.mark("dve", ("yd", j))
            st.ev[("ye", j)] = st.ev[("yd", j)]
        st.run()

    with ExitStack() as pst:
        ps_sum = pst.enter_context(nc.psum_tensor(f"bls{i}", [128, 512], F32))
        ps_sq = pst.enter_context(nc.psum_tensor(f"blq{i}", [128, 512], F32))
        ps_A = pst.enter_context(nc.psum_tensor(f"blA{i}", [128, 512], F32))
        ps_B = pst.enter_context(nc.psum_tensor(f"blB{i}", [128, 512], F32))
        sqb = pst.enter_context(nc.sbuf_tensor(f"bsq{i}", [128, 1024], BF16))
        xbb = pst.enter_context(nc.sbuf_tensor(f"bxb{i}", [128, 1024], BF16))
        rows = pst.enter_context(nc.sbuf_tensor(f"brw{i}", [1, 2048], F32))
        st = Streams(ctx)
        out_fm = scr["out_fm"]
        ytmp = scr["ytmp"]
        m_r, e_r = rows[0:1, 0:512], rows[0:1, 512:1024]
        A_r, B_r = rows[0:1, 1024:1536], rows[0:1, 1536:2048]
        for k in range(NCH):
            sl = k % 2
            if k >= 2:
                st.w("act", "pe", st.ev[("st", k - 2)])
            def f(s, k=k, sl=sl):
                s.activation(sqb[:, sl * 512:(sl + 1) * 512],
                             out_fm[:, k * 512:(k + 1) * 512], AF.Square)
                ctx.inc(s.copy(xbb[:, sl * 512:(sl + 1) * 512],
                               out_fm[:, k * 512:(k + 1) * 512]), "act")
            st.op("act", f)
            st.mark("act", ("sq", k))
            st.w("pe", "act", st.ev[("sq", k)])
            def f(t, k=k, sl=sl):
                t.matmul(ps_sum[0:1, :], sb["onescb"][:, :],
                         xbb[:, sl * 512:(sl + 1) * 512], start=(k == 0),
                         stop=(k == NCH - 1))
                ctx.inc(t.matmul(ps_sq[0:1, :], sb["onescb"][:, :],
                                 sqb[:, sl * 512:(sl + 1) * 512], start=(k == 0),
                                 stop=(k == NCH - 1)), "pe")
            st.op("pe", f)
            st.mark("pe", ("st", k))
        st.w("act", "pe", st.ev[("st", NCH - 1)])
        def f(s):
            s.mul(m_r, ps_sum[0:1, :], 1.0 / D)
            ctx.inc(s.mul(e_r, ps_sq[0:1, :], 1.0 / D), "act")
        st.op("act", f)
        st.mark("act", "me")
        st.w("dve", "act", st.ev["me"])
        def f(v):
            v.tensor_mul(B_r, m_r, m_r)
            ctx.inc(v.tensor_sub(e_r, e_r, B_r), "dve")
        st.op("dve", f)
        st.mark("dve", "var")
        st.w("act", "dve", st.ev["var"])
        def f(s):
            s.activation(A_r, e_r, AF.Ln, bias=EPS)
            ctx.inc(s.activation(A_r, A_r, AF.Exp, scale=-0.5), "act")
        st.op("act", f)
        st.mark("act", "A")
        st.w("dve", "act", st.ev["A"])
        def f(v):
            ctx.inc(v.scalar_tensor_tensor(B_r, m_r, -1.0, A_r, ALU.mult,
                                           ALU.mult), "dve")
        st.op("dve", f)
        st.mark("dve", "B")
        st.w("pe", "dve", st.ev["B"])
        def f(t):
            t.matmul(ps_A[:, :], sb["ones1f"][:, :], A_r, start=True, stop=True)
            ctx.inc(t.matmul(ps_B[:, :], sb["ones1f"][:, :], B_r, start=True,
                             stop=True), "pe")
        st.op("pe", f)
        st.mark("pe", "bc")
        psA3 = ps_A[:, :].rearrange("p (o c) -> p o c", o=1).broadcast_to(
            [128, 4, 512])
        psB3 = ps_B[:, :].rearrange("p (o c) -> p o c", o=1).broadcast_to(
            [128, 4, 512])
        for hf in range(2):
            cs = slice(hf * 2048, (hf + 1) * 2048)
            x3 = out_fm[:, cs].rearrange("p (k c) -> p k c", k=4)
            y3 = ytmp[:, 0:2048].rearrange("p (k c) -> p k c", k=4)
            xs3 = xs[:, cs].rearrange("p (k c) -> p k c", k=4)
            g3 = bpt[:, 56 + hf * 4:56 + hf * 4 + 4].rearrange(
                "p (k o) -> p k o", o=1).broadcast_to([128, 4, 512])
            b3 = bpt[:, 64 + hf * 4:64 + hf * 4 + 4].rearrange(
                "p (k o) -> p k o", o=1).broadcast_to([128, 4, 512])
            st.w("dve", "pe", st.ev["bc"])
            def f(v, x3=x3, y3=y3, xs3=xs3, g3=g3, b3=b3):
                v.tensor_tensor(y3, x3, psA3, ALU.mult)
                v.tensor_tensor(y3, y3, psB3, ALU.add)
                v.tensor_tensor(y3, y3, g3, ALU.mult)
                v.tensor_tensor(y3, y3, b3, ALU.add)
                ctx.inc(v.tensor_tensor(xs3, xs3, y3, ALU.add), "dve")
            st.op("dve", f)
            st.mark("dve", ("xa", hf))
        st.run()


def build(taps=()):
    taps = set(taps)
    nc = bass.Bass("TRN2", target_bir_lowering=False)
    stack = ExitStack()
    ctx = Ctx(nc, stack)
    g = {}
    dr = {}
    g["dram"] = dr

    dr["xT"] = nc.dram_tensor("xT", [D, T], F32, kind="ExternalInput").ap()
    w13h = nc.dram_tensor("w13", [NEXP, 8, 128, 4096], F32,
                          kind="ExternalInput").ap()
    dr["w13"] = [[w13h[e][q] for q in range(8)] for e in range(NEXP)]
    w2h = nc.dram_tensor("w2t", [NEXP, 128, 16384], F32, kind="ExternalInput").ap()
    dr["w2t"] = [w2h[e] for e in range(NEXP)]
    winh = nc.dram_tensor("win", [2, D, D], F32, kind="ExternalInput").ap()
    dr["win"] = [winh[e] for e in range(2)]
    lh = nc.dram_tensor("wlh", [6, 128, 8192], F32, kind="ExternalInput").ap()
    dr["win2"] = lh[0]
    dr["wmid"] = [lh[1], lh[2]]
    dr["w1w"] = lh[3]
    dr["w2w"] = lh[4]
    dr["gT"] = nc.dram_tensor("gT", [D, NEXP], F32, kind="ExternalInput").ap()
    dr["plog"] = nc.dram_tensor("plog", [3, D], F32, kind="ExternalInput").ap()
    dr["b_in"] = nc.dram_tensor("b_in", [3, D], F32, kind="ExternalInput").ap()
    dr["bias_pt"] = nc.dram_tensor("bias_pt", [128, 72], F32,
                                   kind="ExternalInput").ap()
    dr["tri_bf"] = nc.dram_tensor("tri_bf", [128, 128], BF16,
                                  kind="ExternalInput").ap()
    dr["iden_f"] = nc.dram_tensor("iden_f", [128, 128], F32,
                                  kind="ExternalInput").ap()
    dr["iota_row"] = nc.dram_tensor("iota_row", [1, 128], F32,
                                    kind="ExternalInput").ap()
    dr["ones_row_f"] = nc.dram_tensor("ones_row_f", [1, 128], F32,
                                      kind="ExternalInput").ap()
    dr["ones_row_bf"] = nc.dram_tensor("ones_row_bf", [1, 128], BF16,
                                       kind="ExternalInput").ap()
    dr["ones_col_bf"] = nc.dram_tensor("ones_col_bf", [128, 1], BF16,
                                       kind="ExternalInput").ap()
    dr["yT"] = nc.dram_tensor("yT", [D, T], F32, kind="ExternalOutput").ap()
    w13bh = nc.dram_tensor("w13b", [NEXP, 8, 128, 4096], BF16,
                           kind="Internal").ap()
    dr["w13b"] = [[w13bh[e][q] for q in range(8)] for e in range(NEXP)]
    w2bh = nc.dram_tensor("w2b", [NEXP, 128, 16384], BF16, kind="Internal").ap()
    dr["w2b"] = [w2bh[e] for e in range(NEXP)]
    winbh = nc.dram_tensor("winb", [2, D, D], BF16, kind="Internal").ap()
    dr["winb"] = [winbh[e] for e in range(2)]
    lbh = nc.dram_tensor("wlb", [6, 128, 8192], BF16, kind="Internal").ap()
    dr["win2b"] = lbh[0]
    dr["wmidb"] = [lbh[1], lbh[2]]
    dr["w1wb"] = lbh[3]
    dr["w2wb"] = lbh[4]

    tap_out = {}
    TAPS = {
        "PGr": [128, 1024, BF16], "PGi": [128, 1024, BF16],
        "Qr": [128, 1024, BF16], "Qi": [128, 1024, BF16],
        "QLr": [128, 8, F32], "QLi": [128, 8, F32],
        "ln0": [128, 4096, BF16], "crows0": [1, 8192, BF16],
        "postffn1": [128, 4096, F32], "h0": [128, 4096, BF16],
        "hi0": [128, 4096, BF16], "out0": [128, 4096, F32],
        "postattn0": [128, 4096, F32], "postp0": [128, 4096, F32],
    }
    for name in taps:
        shp = TAPS[name]
        tap_out[name] = nc.dram_tensor("tap_" + name, shp[:2], shp[2],
                                       kind="ExternalOutput").ap()

    def tap(name, src):
        if name not in taps:
            return
        st = Streams(ctx)
        def f(eng, s=src, d=tap_out[name]):
            ctx.inc(eng.dma_start(d, s), "out", 16)
        st.op("sp", f)
        st.mark("out", n=16)
        st.run()

    def S(name, shape, dtype):
        g[name] = stack.enter_context(nc.sbuf_tensor(name, shape, dtype))
        return g[name]

    S("xs0", [128, 4096], F32)
    S("xs1", [128, 4096], F32)
    S("xl0", [128, 4096], BF16)
    S("xl1", [128, 4096], BF16)
    S("wAB", [128, 8192], BF16)
    S("wC", [128, 16384], BF16)
    S("wD", [128, 8192], BF16)
    S("wE", [128, 8192], BF16)
    S("PGr", [128, 1024], BF16)
    S("PGi", [128, 1024], BF16)
    S("Qr", [128, 1024], BF16)
    S("Qi", [128, 1024], BF16)
    S("Qineg", [128, 1024], BF16)
    S("tri", [128, 128], BF16)
    S("iden", [128, 128], F32)
    S("iota", [1, 128], F32)
    S("ones1f", [1, 128], F32)
    S("ones1b", [1, 128], BF16)
    S("onescb", [128, 1], BF16)
    S("biaspt", [128, 72], F32)
    S("gTs", [128, 64], F32)
    S("crows", [1, 8192], BF16)
    S("bbrow", [1, 2048], BF16)
    S("scr8", [128, 48], F32)
    S("QLr", [128, 8], F32)
    S("QLi", [128, 8], F32)
    S("lcr", [128, 8], F32)
    S("lci", [128, 8], F32)
    S("bo", [128, 8], F32)
    S("c_eps", [128, 1], F32)
    nc.const_aps.register(F32, EPS, g["c_eps"][:, 0:1])

    st = Streams(ctx)
    pre_e = {}
    pre_e[(0, 0)] = dma_flat(ctx, st, dr["w13"][0][0], 0, 4096, g["wD"], 0,
                             "d0", ndma=2)
    pre_e[(0, 1)] = dma_flat(ctx, st, dr["w13"][0][1], 0, 4096, g["wD"], 4096,
                             "d1", ndma=2)
    for j in range(4):
        w2_0 = dma_flat(ctx, st, dr["w2t"][0], j * 4096, (j + 1) * 4096,
                        g["wC"], j * 4096, "wc")
    win0_ev = dma_cast_w(ctx, st, dr["win"][0], 0, 1024, 0, 1024, g["wAB"], 0,
                         "ab")
    win2_ev_g = dma_flat(ctx, st, dr["win2"], 0, 8192, g["wE"], 0, "wd",
                         ndma=2)
    st.run()

    with ExitStack() as prost:
        emit_prologue(ctx, g, prost)
    for nm in ("PGr", "PGi", "Qr", "Qi", "QLr", "QLi"):
        tap(nm, g[nm][:, :] if nm not in ("QLr", "QLi") else g[nm][:, 0:8])

    xs = [g["xs0"], g["xs1"]]
    xl = [g["xl0"], g["xl1"]]
    load_ev = {0: ctx.n["x"]}
    store_ev = {}

    with ExitStack() as est:
        emit_ln(ctx, g, est, xs[0], xl[0], routed=True, ctile=0,
                wait_x=load_ev[0])
    tap("ln0", g["xl0"][:, :])
    tap("crows0", g["crows"][0:1, :])
    with ExitStack() as est:
        emit_ffn_exp(ctx, g, est, [(xs[0], xl[0], 0)], pre_e, w2_0,
                     mirror=False)
    tap("postffn1", g["xs0"][:, :])

    for p in range(NPASS):
        cur, nxt = p % 2, (p + 1) % 2
        with ExitStack() as est:
            state = {}
            def pre(st_):
                state["win1"] = dma_cast_w(ctx, st_, dr["win"][1], 0, 1024, 0,
                                           1024, g["wD"], 0, "wd",
                                           b16=dr["winb"][1], wkey=("win", 1))
                dma_flat(ctx, st_, dr["wmid"][0], 0, 8192, g["wC"], 0, "wc",
                           b16=dr["wmidb"][0], wkey=("wmid", 0), ndma=2)
                state["wmid"] = dma_flat(ctx, st_, dr["wmid"][1], 0, 8192, g["wC"], 8192, "wc",
                                           b16=dr["wmidb"][1], wkey=("wmid", 1),
                                           ndma=2)
                if p + 1 < NPASS:
                    st_.w("sp", "out", store_ev.get(p - 1))
                    load_ev[p + 1] = dma_x(ctx, st_, dr["xT"], p + 1, xs[nxt])
            with ExitStack() as lnst:
                emit_ln(ctx, g, lnst, xs[cur], xl[cur], routed=False, ctile=0,
                        pre=pre)
            ascr = attn_scratch(ctx, est, xl[nxt], xl[cur])
            with ExitStack() as hst:
                for nme in ("hr", "hin"):
                    ascr[nme] = hst.enter_context(
                        nc.sbuf_tensor(f"a{nme}{ascr['i']}", [128, 4096], BF16))
                emit_attn_A(ctx, g, ascr, xl[cur], win0_ev, state["win1"])
                if p == 0:
                    tap("h0", ascr["hr"][:, :])
                    tap("hi0", ascr["hin"][:, :])
                def pre_b1(st_):
                    scrd = ascr
                    scrd["win2_ev"] = win2_ev_g
                    scrd["w1_ev"] = dma_flat(ctx, st_, dr["w1w"], 0, 8192, g["wAB"], 0, "ab",
                                               b16=dr["w1wb"], wkey=("w1w",),
                                               ndma=2)
                emit_attn_B1(ctx, g, ascr, xl[cur], state["wmid"], pre=pre_b1)
                if p == 0:
                    tap("out0", ascr["out_fm"][:, :])
            nxt_pre = {}
            def pre_b2(st_):
                ascr["w2w_ev"] = dma_flat(ctx, st_, dr["w2w"], 0, 8192, g["wC"], 0, "wc",
                                            b16=dr["w2wb"], wkey=("w2w",),
                                            ndma=2)
                nxt_pre[(0, 0)] = dma_flat(ctx, st_, dr["w13"][0][0], 0, 4096,
                                           g["wD"], 0, "d0",
                                           b16=dr["w13b"][0][0],
                                           wkey=("w13", 0, 0), ndma=2)
                nxt_pre[(0, 1)] = dma_flat(ctx, st_, dr["w13"][0][1], 0, 4096,
                                           g["wD"], 4096, "d1",
                                           b16=dr["w13b"][0][1],
                                           wkey=("w13", 0, 1), ndma=2)
            emit_attn_B2(ctx, g, ascr, xs[cur], ascr["w1_ev"], pre=pre_b2)
            if p == 0:
                tap("postattn0", g["xs0"][:, :])

        with ExitStack() as est:
            w2_0n = [None]
            def pre2(st_):
                for j in range(4):
                    w2_0n[0] = dma_flat(ctx, st_, dr["w2t"][0], j * 4096,
                                        (j + 1) * 4096, g["wC"], j * 4096,
                                        "wc", b16=dr["w2b"][0],
                                        wkey=("w2", 0, j))
            with ExitStack() as lnst:
                emit_ln(ctx, g, lnst, xs[cur], xl[cur], routed=True, ctile=0,
                        pre=pre2)
            tiles = [(xs[cur], xl[cur], 0)]
            if p + 1 < NPASS:
                with ExitStack() as lnst:
                    emit_ln(ctx, g, lnst, xs[nxt], xl[nxt], routed=True,
                            ctile=1, wait_x=load_ev[p + 1])
                tiles.append((xs[nxt], xl[nxt], 1))
            win0n = [None]
            def post(st_):
                st_.w("sp", "dve", st_.ev[("xa", NEXP - 1, 0, 7)])
                store_ev[p] = dma_x(ctx, st_, dr["yT"], p, xs[cur], store=True)
                if p + 1 < NPASS:
                    win0n[0] = dma_cast_w(ctx, st_, dr["win"][0], 0, 1024, 0,
                                          1024, g["wAB"], 0, "ab",
                                          b16=dr["winb"][0], wkey=("win", 0))
            emit_ffn_exp(ctx, g, est, tiles, nxt_pre, w2_0n[0], post=post)
            if win0n[0] is not None:
                win0_ev = win0n[0]
        if p == 0:
            tap("postp0", g["xs0"][:, :])

    st = Streams(ctx)
    def f(sp):
        sp.wait_ge(ctx.sem["out"], ctx.n["out"])
    st.op("sp", f)
    st.run()
    stack.close()
    return nc


def prep_inputs(L0_attn_x, L0_attn_W_in, L0_attn_b_in, L0_attn_params_log,
                L0_attn_W_mid, L0_attn_b_mid, L0_attn_w1_w, L0_attn_w1_b,
                L0_attn_w2_w, L0_attn_w2_b, L0_attn_ln_g, L0_attn_ln_b,
                L0_ffn_gate_w, L0_ffn_w13_weight, L0_ffn_w2_weight):
    f = lambda a: np.asarray(a, dtype=np.float32)
    x = f(L0_attn_x)
    w13 = np.ascontiguousarray(np.transpose(f(L0_ffn_w13_weight), (0, 2, 1)))
    order = []
    for m in range(16):
        order.extend(range(m * 128, (m + 1) * 128))
        order.extend(range(2048 + m * 128, 2048 + (m + 1) * 128))
    w13 = w13[:, :, order]
    w13 = w13.reshape(8, 8, 128, 8, 2, 2, 128).transpose(0, 3, 2, 4, 5, 1, 6)
    w13 = np.ascontiguousarray(w13.reshape(8, 8, 128, 4096))
    w2t = np.transpose(f(L0_ffn_w2_weight), (0, 2, 1))
    w2t = w2t.reshape(8, 16, 128, 8, 128).transpose(0, 2, 3, 1, 4)
    w2t = np.ascontiguousarray(w2t.reshape(8, 128, 16384))

    def lhsTlay(W):
        return np.ascontiguousarray(
            W.reshape(8, 128, 8, 128).transpose(1, 2, 0, 3).reshape(128, 8192))
    b_in = f(L0_attn_b_in)
    b_mid = f(L0_attn_b_mid)
    bias_pt = np.zeros((128, 72), np.float32)
    vecs = [b_in[0], b_in[1], b_in[2], b_mid[0], b_mid[1], f(L0_attn_w1_b),
            f(L0_attn_w2_b), f(L0_attn_ln_g), f(L0_attn_ln_b)]
    for i, v in enumerate(vecs):
        bias_pt[:, i * 8:(i + 1) * 8] = v.reshape(8, 128).T
    wlh = np.stack([
        lhsTlay(f(L0_attn_W_in)[2]),
        lhsTlay(f(L0_attn_W_mid)[0]),
        lhsTlay(f(L0_attn_W_mid)[1]),
        lhsTlay(f(L0_attn_w1_w)),
        lhsTlay(f(L0_attn_w2_w)),
        np.zeros((128, 8192), np.float32),
    ])
    shared = dict(
        w13=w13, w2t=w2t,
        win=np.ascontiguousarray(f(L0_attn_W_in)[0:2]),
        wlh=wlh,
        gT=np.ascontiguousarray(f(L0_ffn_gate_w).T),
        plog=np.ascontiguousarray(f(L0_attn_params_log)),
        b_in=np.ascontiguousarray(b_in),
        bias_pt=bias_pt,
        tri_bf=np.triu(np.ones((128, 128), np.float32)).astype(ml_dtypes.bfloat16),
        iden_f=np.eye(128, dtype=np.float32),
        iota_row=np.arange(128, dtype=np.float32).reshape(1, 128),
        ones_row_f=np.ones((1, 128), np.float32),
        ones_row_bf=np.ones((1, 128), ml_dtypes.bfloat16),
        ones_col_bf=np.ones((128, 1), ml_dtypes.bfloat16),
    )
    in_maps = []
    for c in range(8):
        m = dict(shared)
        m["xT"] = np.ascontiguousarray(x[c].T)
        in_maps.append(m)
    return in_maps


def kernel(**inputs):
    in_maps = prep_inputs(**inputs)
    nc = build()
    res = run_bass_kernel_spmd(nc, in_maps, core_ids=list(range(8)))
    return np.stack([np.ascontiguousarray(r["yT"].T) for r in res.results], axis=0)
